# version 37
# baseline (speedup 1.0000x reference)
import numpy as np
from contextlib import ExitStack

import concourse.bass as bass
import concourse.mybir as mybir
from concourse.bass_utils import run_bass_kernel_spmd

F32 = mybir.dt.float32
BF16 = mybir.dt.bfloat16
I8 = mybir.dt.int8
AF = mybir.ActivationFunctionType
ALU = mybir.AluOpType
AXX = mybir.AxisListType.X

B, L, C, H, D = 4, 4096, 1024, 16, 64
TOWN = 2048
NCORE = 8
C3 = 12582912.0
EPS = 1e-5
NTB = TOWN // 128
NSB = L // 128


def _slopes():
    n = H
    start = 2.0 ** (-(2.0 ** (-(np.log2(n) - 3))))
    return np.array([start * start ** i for i in range(n)], dtype=np.float32)


def _subtiles(sb):
    s0 = sb * 128
    pieces = []
    if s0 >= TOWN:
        pieces.append(("up", 0, TOWN))
    else:
        if s0 > 0:
            pieces.append(("up", 0, s0))
        pieces.append(("diag", s0, s0 + 128))
        if s0 + 128 < TOWN:
            pieces.append(("low", s0 + 128, TOWN))
    res = []
    for kind, lo, hi in pieces:
        a = lo
        while a < hi:
            b2 = min(hi, (a // 512 + 1) * 512)
            res.append((kind, a, b2))
            a = b2
    return res


class Sem:
    def __init__(self, ctx, nc, name):
        self.h = ctx.enter_context(nc.semaphore(name))
        self.v = 0


class Phase:
    ENGS = ("sync", "tensor", "vector", "scalar", "gpsimd")

    def __init__(self, nc):
        self.nc = nc
        self.q = {e: [] for e in self.ENGS}

    def on(self, eng, fn):
        self.q[eng].append(fn)

    def emit(self):
        with self.nc.Block() as blk:
            for e in self.ENGS:
                if self.q[e]:
                    getattr(blk, e)(lambda eng, _f=tuple(self.q[e]): [f(eng) for f in _f])


def build_nc(upto=6, dbg=False):
    slopes = [float(s) for s in _slopes()]
    nc = bass.Bass("TRN2", target_bir_lowering=False)

    xf = nc.dram_tensor("xf", [L, C], F32, kind="ExternalInput")
    xft = nc.dram_tensor("xft", [C, L], F32, kind="ExternalInput")
    wqt = nc.dram_tensor("wqt", [C, C], F32, kind="ExternalInput")
    wkt = nc.dram_tensor("wkt", [C, C], F32, kind="ExternalInput")
    wvt = nc.dram_tensor("wvt", [C, C], F32, kind="ExternalInput")
    wot = nc.dram_tensor("wot", [C, C], F32, kind="ExternalInput")
    bo = nc.dram_tensor("bo", [1, C], F32, kind="ExternalInput")
    wf8 = nc.dram_tensor("wf8", [C, C], I8, kind="ExternalInput")
    wfs = nc.dram_tensor("wfs", [1, C], F32, kind="ExternalInput")
    fbi = nc.dram_tensor("fbi", [1, C], F32, kind="ExternalInput")
    pos = nc.dram_tensor("pos", [1, L], F32, kind="ExternalInput")
    dgi = nc.dram_tensor("dgi", [128, 128], F32, kind="ExternalInput")
    coef = nc.dram_tensor("coef", [32, 1], F32, kind="ExternalInput")
    out = nc.dram_tensor("out", [TOWN, C], F32, kind="ExternalOutput")

    spill = "ExternalOutput" if dbg else "Internal"
    ktsH = nc.dram_tensor("ktsH", [C, L], BF16, kind=spill)
    ktsL = nc.dram_tensor("ktsL", [C, L], BF16, kind=spill)
    vsp = nc.dram_tensor("vsp", [L, C], F32, kind=spill)
    x2s = nc.dram_tensor("x2s", [TOWN, C], F32, kind=spill)
    ccin = nc.dram_tensor("ccin", [128, 128], F32, kind="Internal")
    ccout = nc.dram_tensor("ccout", [128, 128], F32, kind="Internal")
    gtmp = nc.dram_tensor("gtmp", [128], F32, kind="Internal")
    recd = nc.dram_tensor("recd", [16, TOWN], F32, kind="Internal")
    qtd = nc.dram_tensor("qtd", [128, 8 * TOWN], F32, kind="ExternalOutput") if dbg else None
    rcd = nc.dram_tensor("rcd", [16, TOWN], F32, kind="ExternalOutput") if dbg else None

    ctx = ExitStack()
    with ctx:
        ones1 = ctx.enter_context(nc.sbuf_tensor("ones1", [1, 128], F32))
        sS = ctx.enter_context(nc.sbuf_tensor("sS", [1, 1], F32))
        sRi = ctx.enter_context(nc.sbuf_tensor("sRi", [1, 1], F32))
        sRb = ctx.enter_context(nc.sbuf_tensor("sRb", [128, 1], F32))
        sSb = ctx.enter_context(nc.sbuf_tensor("sSb", [128, 1], F32))
        amaxc = ctx.enter_context(nc.sbuf_tensor("amaxc", [128, NTB], F32))
        sEps = ctx.enter_context(nc.sbuf_tensor("sEps", [128, 1], F32))
        qts = ctx.enter_context(ExitStack())
        QT = qts.enter_context(nc.sbuf_tensor("QT", [128, 8 * TOWN], F32))
        sdg = qts.enter_context(nc.sbuf_tensor("sdg", [128, 128], F32))
        dnall = qts.enter_context(nc.sbuf_tensor("dnall", [128, TOWN], F32))
        AUGT = qts.enter_context(nc.sbuf_tensor("AUGT", [96, L], BF16))
        AUG1, AUG2, AUG3 = AUGT[0:32], AUGT[32:64], AUGT[64:96]
        onesb2 = qts.enter_context(nc.sbuf_tensor("onesb2", [33, TOWN], BF16))
        onesb = onesb2[0:1]
        negb = onesb2[32:33]
        p0s = ExitStack()
        RAMP = p0s.enter_context(nc.sbuf_tensor("RAMP", [32, L], F32))
        spos = p0s.enter_context(nc.sbuf_tensor("spos", [1, L], F32))
        rs1 = p0s.enter_context(nc.sbuf_tensor("rs1", [32, L], F32))
        rs2 = p0s.enter_context(nc.sbuf_tensor("rs2", [32, L], F32))
        scoef = p0s.enter_context(nc.sbuf_tensor("scoef", [32, 1], F32))
        augt2 = p0s.enter_context(nc.sbuf_tensor("augt2", [32, L], BF16))
        augt3 = p0s.enter_context(nc.sbuf_tensor("augt3", [32, L], BF16))

        s0ld = Sem(ctx, nc, "s0ld")
        s0t = Sem(ctx, nc, "s0t")
        s0v = Sem(ctx, nc, "s0v")
        s0d = Sem(ctx, nc, "s0d")
        ph = Phase(nc)

        def _p0_sync(e):
            for dst, src in ((spos, pos), (sdg, dgi), (scoef, coef)):
                e.dma_start(out=dst[:], in_=src[:]).then_inc(s0ld.h, 16)
        ph.on("sync", _p0_sync)
        s0ld.v = 48

        def _p0_vec(e):
            e.memset(ones1[:], 1.0)
            e.memset(sEps[:], EPS)
        ph.on("vector", _p0_vec)
        ph.emit()

        with nc.psum_tensor("pbc0", [32, 2048], F32) as pbc0:
            ph = Phase(nc)

            def _p0b_t(e):
                e.wait_ge(s0ld.h, 48)
                for half in range(2):
                    if half == 1:
                        e.wait_ge(s0v.h, 1)
                    for j in range(4):
                        e.matmul(pbc0[:, j * 512:(j + 1) * 512], ones1[0:1, 0:32],
                                 spos[0:1, half * 2048 + j * 512:half * 2048 + (j + 1) * 512],
                                 start=True, stop=True).then_inc(s0t.h, 1)
            ph.on("tensor", _p0b_t)
            s0t.v = 8

            def _p0b_v(e):
                e.memset(onesb2[0:1, :], 1.0)
                e.memset(onesb2[32:33, :], -1.0)
                for half in range(2):
                    e.wait_ge(s0t.h, 4 * (half + 1))
                    e.tensor_scalar(RAMP[:, half * 2048:(half + 1) * 2048], pbc0[:],
                                    scoef[:], None, ALU.mult).then_inc(s0v.h, 1)
                e.tensor_copy(AUG1[:], RAMP[:])
                e.tensor_tensor(rs1[:], RAMP[:], AUG1[:], ALU.subtract)
                e.tensor_copy(augt2[:], rs1[:])
                e.tensor_tensor(rs2[:], rs1[:], augt2[:], ALU.subtract)
                e.tensor_copy(augt3[:], rs2[:]).then_inc(s0v.h, 1)
            ph.on("vector", _p0b_v)
            s0v.v = 3

            def _p0b_s(e):
                e.wait_ge(s0v.h, 3)
                e.dma_start(out=AUGT[32:64, :], in_=augt2[:]).then_inc(s0d.h, 16)
                e.dma_start(out=AUGT[64:96, :], in_=augt3[:]).then_inc(s0d.h, 16)
            ph.on("scalar", _p0b_s)
            s0d.v = 32
            ph.emit()
        p0s.close()

        with ExitStack() as p2:
            WATH = [p2.enter_context(nc.sbuf_tensor(f"WATH{i}", [128, 8192], BF16))
                    for i in range(2)]
            WATL = [p2.enter_context(nc.sbuf_tensor(f"WATL{i}", [128, 8192], BF16))
                    for i in range(2)]
            wldf = p2.enter_context(nc.sbuf_tensor("wldf", [128, 1024], F32))
            xTF = p2.enter_context(nc.sbuf_tensor("xTF", [128, 4096], F32))
            xTH = p2.enter_context(nc.sbuf_tensor("xTH", [128, 8192], BF16))
            xTL = p2.enter_context(nc.sbuf_tensor("xTL", [128, 8192], BF16))
            kstH = p2.enter_context(nc.sbuf_tensor("kstH", [128, 512], BF16))
            vst = p2.enter_context(nc.sbuf_tensor("vst", [128, 512], F32))
            kstL = p2.enter_context(nc.sbuf_tensor("kstL", [128, 512], BF16))
            Kp = [p2.enter_context(nc.psum_tensor(f"Kp{i}", [128, 512], F32)) for i in range(2)]
            Vp = [p2.enter_context(nc.psum_tensor(f"Vp{i}", [128, 1024], F32)) for i in range(2)]

            s_lw = Sem(ctx, nc, "s_lw")
            s_lx = Sem(ctx, nc, "s_lx")
            s_ws = Sem(ctx, nc, "s_ws")
            s_xs = Sem(ctx, nc, "s_xs")
            s_mk = Sem(ctx, nc, "s_mk")
            s_ek = Sem(ctx, nc, "s_ek")
            s_wk = Sem(ctx, nc, "s_wk")
            s_mv = Sem(ctx, nc, "s_mv")
            s_ev = Sem(ctx, nc, "s_ev")
            s_wv = Sem(ctx, nc, "s_wv")

            ph = Phase(nc)
            wsrc_map = {"wk": wkt, "wv": wvt, "wq": wqt}

            def sched_weight_load(wname, wbuf, after_mk, after_mv):
                wsrc = wsrc_map[wname]
                ws_base = s_ws.v

                def f_sync(e, wsrc=wsrc, after_mk=after_mk, after_mv=after_mv,
                           ws_base=ws_base, first=(s_lw.v == 0)):
                    if first:
                        e.wait_ge(s0d.h, 32)
                    for ci in range(8):
                        if ws_base + ci > 0:
                            e.wait_ge(s_ws.h, ws_base + ci)
                        e.dma_start(out=wldf[:],
                                    in_=wsrc[ci * 128:(ci + 1) * 128, :]).then_inc(s_lw.h, 16)
                ph.on("scalar", f_sync)
                lw_base = s_lw.v
                s_lw.v += 16 * 8

                def f_vsp(e, lw_base=lw_base, wbuf=wbuf,
                          after_mk=after_mk, after_mv=after_mv):
                    if after_mk:
                        e.wait_ge(s_mk.h, after_mk)
                    if after_mv:
                        e.wait_ge(s_mv.h, after_mv)
                    for ci in range(8):
                        e.wait_ge(s_lw.h, lw_base + 16 * (ci + 1))
                        e.tensor_copy(WATH[wbuf][:, ci * 1024:(ci + 1) * 1024],
                                      wldf[:])
                        e.tensor_tensor(WATL[wbuf][:, ci * 1024:(ci + 1) * 1024],
                                        wldf[:],
                                        WATH[wbuf][:, ci * 1024:(ci + 1) * 1024],
                                        ALU.subtract).then_inc(s_ws.h, 1)
                ph.on("vector", f_vsp)
                s_ws.v += 8

            def sched_tile(ti, tt, mode, wbuf):
                slot = ti & 1
                t0 = tt * 512
                mk_prev2 = max(0, s_mk.v - (8 if mode in ("k", "q") else 0) * 1)
                need_mk = tile_mk.get(ti - 2, 0)
                need_mv = tile_mv.get(ti - 2, 0)

                xs0 = s_xs.v
                for hf in range(2):
                    fslot = (2 * ti + hf) & 1
                    def f_sync(e, t0=t0, hf=hf, fslot=fslot,
                               xs_need=max(0, s_xs.v - 1)):
                        if xs_need:
                            e.wait_ge(s_xs.h, xs_need)
                        for ci4 in range(4):
                            ci = hf * 4 + ci4
                            e.dma_start(out=xTF[:, fslot * 2048 + ci4 * 512:
                                                fslot * 2048 + (ci4 + 1) * 512],
                                        in_=xft[ci * 128:(ci + 1) * 128, t0:t0 + 512]
                                        ).then_inc(s_lx.h, 16)
                    ph.on("scalar", f_sync)
                    s_lx.v += 16 * 4
                    lx_need = s_lx.v

                    def f_xsp(e, slot=slot, hf=hf, fslot=fslot, lx_need=lx_need,
                              need_mk=need_mk, need_mv=need_mv):
                        e.wait_ge(s_lx.h, lx_need)
                        if hf == 0:
                            if need_mk:
                                e.wait_ge(s_mk.h, need_mk)
                            if need_mv:
                                e.wait_ge(s_mv.h, need_mv)
                        o = slot * 4096 + hf * 2048
                        fo = fslot * 2048
                        e.tensor_copy(xTH[:, o:o + 2048], xTF[:, fo:fo + 2048])
                        e.tensor_tensor(xTL[:, o:o + 2048], xTF[:, fo:fo + 2048],
                                        xTH[:, o:o + 2048],
                                        ALU.subtract).then_inc(s_xs.h, 1)
                    ph.on("vector", f_xsp)
                    s_xs.v += 1
                xs_need2 = s_xs.v
                ws_need = s_ws.v

                if mode in ("k", "q"):
                    for fj in range(8):
                        kk = s_mk.v
                        pslot = kk & 1

                        def f_t(e, fj=fj, slot=slot, pslot=pslot, xs_need2=xs_need2,
                                ws_need=ws_need, ek_need=max(0, s_ek.v - 1)):
                            e.wait_ge(s_xs.h, xs_need2)
                            e.wait_ge(s_ws.h, ws_need)
                            if ek_need:
                                e.wait_ge(s_ek.h, ek_need)
                            np_ = 0
                            for ci in range(8):
                                for wa, xb in ((WATH[wbuf], xTH), (WATH[wbuf], xTL),
                                               (WATL[wbuf], xTH)):
                                    np_ += 1
                                    ins = e.matmul(
                                        Kp[pslot][:],
                                        wa[:, ci * 1024 + fj * 128:ci * 1024 + (fj + 1) * 128],
                                        xb[:, slot * 4096 + ci * 512:slot * 4096 + (ci + 1) * 512],
                                        start=(np_ == 1), stop=(np_ == 24))
                                    if np_ == 24:
                                        ins.then_inc(s_mk.h, 1)
                        ph.on("tensor", f_t)
                        s_mk.v += 1

                        if mode == "q":
                            def f_v(e, tt=tt, fj=fj, pslot=pslot, mk_need=s_mk.v):
                                e.wait_ge(s_mk.h, mk_need)
                                e.tensor_scalar(
                                    QT[:, fj * TOWN + tt * 512:fj * TOWN + tt * 512 + 512],
                                    Kp[pslot][:], 0.125, None, ALU.mult).then_inc(s_ek.h, 1)
                            ph.on("vector", f_v)
                            s_ek.v += 1
                        else:
                            def f_v(e, pslot=pslot, mk_need=s_mk.v,
                                    wk_need=s_wk.v):
                                e.wait_ge(s_mk.h, mk_need)
                                if wk_need:
                                    e.wait_ge(s_wk.h, wk_need)
                                e.tensor_copy(kstH[:], Kp[pslot][:])
                                e.tensor_tensor(kstL[:], Kp[pslot][:], kstH[:],
                                                ALU.subtract).then_inc(s_ek.h, 1)
                            ph.on("vector", f_v)
                            s_ek.v += 1

                            def f_s(e, fj=fj, pslot=pslot, ek_need=s_ek.v, t0=t0):
                                e.wait_ge(s_ek.h, ek_need)
                                e.dma_start(out=ktsH[fj * 128:(fj + 1) * 128, t0:t0 + 512],
                                            in_=kstH[:]).then_inc(s_wk.h, 16)
                                e.dma_start(out=ktsL[fj * 128:(fj + 1) * 128, t0:t0 + 512],
                                            in_=kstL[:]).then_inc(s_wk.h, 16)
                            ph.on("sync", f_s)
                            s_wk.v += 32

                if mode == "v":
                    for r in range(4):
                        vv = s_mv.v
                        pslot = vv & 1

                        def f_t(e, r=r, slot=slot, pslot=pslot, xs_need2=xs_need2,
                                ws_need=ws_need, ev_need=max(0, 2 * (s_mv.v - 1))):
                            e.wait_ge(s_xs.h, xs_need2)
                            e.wait_ge(s_ws.h, ws_need)
                            if ev_need:
                                e.wait_ge(s_ev.h, ev_need)
                            for hh in range(2):
                                np_ = 0
                                for ci in range(8):
                                    for xa, wb in ((xTH, WATH[wbuf]), (xTH, WATL[wbuf]),
                                                   (xTL, WATH[wbuf])):
                                        np_ += 1
                                        ins = e.matmul(
                                            Vp[pslot][:, hh * 512:(hh + 1) * 512],
                                            xa[:, slot * 4096 + ci * 512 + r * 128:slot * 4096 + ci * 512 + (r + 1) * 128],
                                            wb[:, ci * 1024 + hh * 512:ci * 1024 + (hh + 1) * 512],
                                            start=(np_ == 1), stop=(np_ == 24))
                                        if np_ == 24 and hh == 1:
                                            ins.then_inc(s_mv.h, 1)
                        ph.on("tensor", f_t)
                        s_mv.v += 1

                        for vh in range(2):
                            def f_v(e, pslot=pslot, vh=vh, mv_need=s_mv.v,
                                    wv_need=s_wv.v):
                                e.wait_ge(s_mv.h, mv_need)
                                if wv_need:
                                    e.wait_ge(s_wv.h, wv_need)
                                e.tensor_copy(vst[:], Vp[pslot][:, vh * 512:(vh + 1) * 512]
                                              ).then_inc(s_ev.h, 1)
                            ph.on("vector", f_v)
                            s_ev.v += 1

                            def f_s(e, t0=t0, r=r, vh=vh, ev_need=s_ev.v):
                                e.wait_ge(s_ev.h, ev_need)
                                e.dma_start(out=vsp[t0 + r * 128:t0 + (r + 1) * 128,
                                                    vh * 512:(vh + 1) * 512],
                                            in_=vst[:]).then_inc(s_wv.h, 16)
                            ph.on("sync", f_s)
                            s_wv.v += 16
                tile_mk[ti] = s_mk.v
                tile_mv[ti] = s_mv.v

            tile_mk, tile_mv = {}, {}
            ti = 0
            buf_last = {0: (0, 0), 1: (0, 0)}
            for w, (wname, mode) in enumerate((("wk", "k"), ("wv", "v"), ("wq", "q"))):
                wbuf = w & 1
                amk, amv = buf_last[wbuf]
                sched_weight_load(wname, wbuf, amk, amv)
                for tt in range(8 if mode in ("k", "v") else 4):
                    sched_tile(ti, tt, mode, wbuf)
                    ti += 1
                buf_last[wbuf] = (s_mk.v, s_mv.v)

            def f_fin2(e):
                e.wait_ge(s_wk.h, s_wk.v)
                e.wait_ge(s_wv.h, s_wv.v)
            ph.on("sync", f_fin2)
            ph.emit()

        if dbg and upto == 2:
            _dump(nc, ctx, [(qtd, QT)])
        if upto == 2:
            return nc

        with ExitStack() as p3:
            kbufH = [p3.enter_context(nc.sbuf_tensor(f"kbufH{i}", [128, L], BF16)) for i in range(2)]
            kbufL = [p3.enter_context(nc.sbuf_tensor(f"kbufL{i}", [128, L], BF16)) for i in range(2)]
            qbufH = [p3.enter_context(nc.sbuf_tensor(f"qbufH{i}", [128, TOWN], BF16)) for i in range(2)]
            qbufL = [p3.enter_context(nc.sbuf_tensor(f"qbufL{i}", [128, TOWN], BF16)) for i in range(2)]
            qtmpF = p3.enter_context(nc.sbuf_tensor("qtmpF", [64, TOWN], F32))
            vbuf = [p3.enter_context(nc.sbuf_tensor(f"vbuf{i}", [128, NSB * 65], F32)) for i in range(2)]
            Abuf = [[p3.enter_context(nc.sbuf_tensor(f"A{i}{j}", [128, 1024], F32))
                     for j in range(2)] for i in range(2)]
            sdp = p3.enter_context(nc.sbuf_tensor("sdp", [128, 1920], F32))
            Sp = [p3.enter_context(nc.psum_tensor(f"Sp{i}", [128, 1024], F32)) for i in range(2)]
            Gp = p3.enter_context(nc.psum_tensor("Gp", [65, 2048], F32))

            s_sts = Sem(ctx, nc, "s_sts")
            s_stg = Sem(ctx, nc, "s_stg")
            s_stc = Sem(ctx, nc, "s_stc")
            s_qsp = Sem(ctx, nc, "s_qsp")
            s_fn = Sem(ctx, nc, "s_fn")
            ek_full = s_ek.v
            s_qk = Sem(ctx, nc, "s_qk")
            s_dg = Sem(ctx, nc, "s_dg")
            s_ex = Sem(ctx, nc, "s_ex")
            s_av = Sem(ctx, nc, "s_av")
            s_gd = Sem(ctx, nc, "s_gd")
            s_dn = Sem(ctx, nc, "s_dn")
            s_hd = Sem(ctx, nc, "s_hd")

            _dgc = {}
            _c = 2 * NSB + 5
            for _h in range(H):
                _c += 1
                for _sb in (list(range(16, NSB)) + list(range(16))):
                    for _kind, _lo, _hi in _subtiles(_sb):
                        if _kind == "diag":
                            _c += 1
                            _dgc[(_h, _sb, _lo // 1024)] = _c
            init_dg = 2 * NSB + 5

            SBORD = list(range(16, NSB)) + list(range(16))
            SBPOS = {sb: i for i, sb in enumerate(SBORD)}

            def ex_idx(h, sb, lh):
                return (h * NSB + SBPOS[sb]) * 2 + lh + 1

            ph = Phase(nc)

            def f_init(e):
                e.memset(sdp[:], 0.0).then_inc(s_dg.h, 1)
                e.wait_ge(s0v.h, 1)
                for i in range(2):
                    e.memset(kbufL[i][64:96, :], 0.0).then_inc(s_dg.h, 1)
                    e.memset(qbufL[i][64:96, :], 0.0).then_inc(s_dg.h, 1)
                    for r, src in ((66, negb), (67, negb), (70, onesb), (71, onesb)):
                        for hf in range(2):
                            e.dma_start(out=kbufH[i][r:r + 1, hf * TOWN:(hf + 1) * TOWN],
                                        in_=src[:]).then_inc(s_stc.h, 16)
                    for r, src in ((64, onesb), (65, onesb), (68, negb), (69, negb)):
                        e.dma_start(out=qbufH[i][r:r + 1, :], in_=src[:]
                                    ).then_inc(s_stc.h, 16)
                for i in range(2):
                    for cc in range(NSB):
                        e.memset(vbuf[i][:, cc * 65 + 64:cc * 65 + 65], 1.0
                                 ).then_inc(s_dg.h, 1)
            ph.on("gpsimd", f_init)
            s_stc.v += 16 * 24
            s_dg.v += 2 * NSB + 5

            stage_sts, stage_stg, stage_qsp = {}, {}, {}

            def sched_stage(h):
                par = h & 1
                hd_need = max(0, h - 1)

                def f_sync(e, h=h, par=par, hd_need=hd_need):
                    if hd_need:
                        e.wait_ge(s_hd.h, hd_need)
                    e.dma_start(out=kbufH[par][0:64, :],
                                in_=ktsH[h * 64:(h + 1) * 64, :]).then_inc(s_sts.h, 16)
                    e.dma_start(out=kbufL[par][0:64, :],
                                in_=ktsL[h * 64:(h + 1) * 64, :]).then_inc(s_sts.h, 16)
                    for cc in range(NSB):
                        e.dma_start(out=vbuf[par][:, cc * 65:cc * 65 + 64],
                                    in_=vsp[cc * 128:(cc + 1) * 128, h * 64:(h + 1) * 64]
                                    ).then_inc(s_sts.h, 16)
                ph.on("sync", f_sync)
                s_sts.v += 16 * (2 + NSB)
                stage_sts[h] = s_sts.v

                def f_gp(e, h=h, par=par, hd_need=hd_need,
                         stage_stg_base=s_stg.v):
                    if hd_need:
                        e.wait_ge(s_hd.h, hd_need)
                    if h < 2:
                        e.wait_ge(s0v.h, 3)
                        e.wait_ge(s0d.h, 32)
                        e.wait_ge(s_ek.h, ek_full)
                    for v2, (rk, rq) in ((0, (64, 66)), (1, (68, 70))):
                        e.dma_start(out=kbufH[par][rk:rk + 1, :],
                                    in_=AUG1[2 * h + v2:2 * h + v2 + 1, :]).then_inc(s_stg.h, 16)
                        e.dma_start(out=kbufH[par][rk + 1:rk + 2, :],
                                    in_=AUG3[2 * h + v2:2 * h + v2 + 1, :]).then_inc(s_stg.h, 16)
                        e.dma_start(out=kbufL[par][rk:rk + 1, :],
                                    in_=AUG2[2 * h + v2:2 * h + v2 + 1, :]).then_inc(s_stg.h, 16)
                        e.dma_start(out=qbufH[par][rq:rq + 1, :],
                                    in_=AUG1[2 * h + v2:2 * h + v2 + 1, 0:TOWN]).then_inc(s_stg.h, 16)
                        e.dma_start(out=qbufH[par][rq + 1:rq + 2, :],
                                    in_=AUG3[2 * h + v2:2 * h + v2 + 1, 0:TOWN]).then_inc(s_stg.h, 16)
                        e.dma_start(out=qbufL[par][rq:rq + 1, :],
                                    in_=AUG2[2 * h + v2:2 * h + v2 + 1, 0:TOWN]).then_inc(s_stg.h, 16)
                    if h % 2:
                        e.dma_start(out=qtmpF[:],
                                    in_=QT[64:128, (h // 2) * TOWN:(h // 2 + 1) * TOWN]
                                    ).then_inc(s_stg.h, 16)
                ph.on("gpsimd", f_gp)
                s_stg.v += 16 * (12 + (h % 2))
                stage_stg[h] = s_stg.v

                def f_qs(e, h=h, par=par, hd_need=hd_need, stg_need=stage_stg[h],
                         ek_need=(ek_full if h < 2 else 0)):
                    if hd_need:
                        e.wait_ge(s_hd.h, hd_need)
                    if ek_need:
                        e.wait_ge(s_ek.h, ek_need)
                    if h % 2:
                        e.wait_ge(s_stg.h, stg_need)
                        qsrc = qtmpF[:]
                    else:
                        qsrc = QT[0:64, (h // 2) * TOWN:(h // 2 + 1) * TOWN]
                    e.tensor_copy(qbufH[par][0:64, :], qsrc)
                    e.tensor_tensor(qbufL[par][0:64, :], qsrc, qbufH[par][0:64, :],
                                    ALU.subtract).then_inc(s_qsp.h, 1)
                ph.on("vector", f_qs)
                s_qsp.v += 1
                stage_qsp[h] = s_qsp.v

            sched_stage(0)
            for h in range(H):
                if h + 1 < H:
                    sched_stage(h + 1)
                par = h & 1

                def f_t(e, h=h, par=par, sts_need=stage_sts[h], stg_need=stage_stg[h],
                        qsp_need=stage_qsp[h]):
                    e.wait_ge(s_sts.h, sts_need)
                    e.wait_ge(s_stg.h, stg_need)
                    e.wait_ge(s_qsp.h, qsp_need)
                    if h == 0:
                        e.wait_ge(s_dg.h, init_dg)
                        e.wait_ge(s_stc.h, 16 * 24)
                    for sb in SBORD:
                        subs = _subtiles(sb)
                        for lh in range(2):
                            prev = ex_idx(h, sb, lh) - 2
                            if prev > 0:
                                e.wait_ge(s_ex.h, prev)
                            last = None
                            for kind, lo, hi in subs:
                                lo2, hi2 = max(lo, lh * 1024), min(hi, (lh + 1) * 1024)
                                if lo2 >= hi2:
                                    continue
                                out_ap = Sp[lh][:, lo2 - lh * 1024:hi2 - lh * 1024]
                                ka = 64 if kind == "diag" else (68 if kind == "low" else 72)
                                ksl = slice(sb * 128, (sb + 1) * 128)
                                e.matmul(out_ap,
                                         kbufH[par][0:ka, ksl],
                                         qbufH[par][0:ka, lo2:hi2],
                                         start=True, stop=False)
                                e.matmul(out_ap,
                                         kbufH[par][0:ka, ksl],
                                         qbufL[par][0:ka, lo2:hi2],
                                         start=False, stop=False, skip_group_check=True)
                                last = e.matmul(out_ap,
                                                kbufL[par][0:ka, ksl],
                                                qbufH[par][0:ka, lo2:hi2],
                                                start=False, stop=True, skip_group_check=True)
                            last.then_inc(s_qk.h, 1)
                        for lh in range(2):
                            if sb == SBORD[0] and h >= 1:
                                e.wait_ge(s_gd.h, h)
                            e.wait_ge(s_ex.h, ex_idx(h, sb, lh))
                            for jj in range(2):
                                ins = e.matmul(
                                    Gp[0:65, lh * 1024 + jj * 512:lh * 1024 + (jj + 1) * 512],
                                    vbuf[par][:, sb * 65:sb * 65 + 65],
                                    Abuf[sb & 1][lh][:, jj * 512:(jj + 1) * 512],
                                    start=(sb == SBORD[0]), stop=(sb == SBORD[-1]))
                                if jj == 1:
                                    ins.then_inc(s_av.h, 1)
                    e.nop().then_inc(s_hd.h, 1)
                ph.on("tensor", f_t)
                s_qk.v += 2 * NSB
                s_av.v += 2 * NSB
                s_hd.v += 1

                def f_v(e, h=h):
                    if h == 0:
                        e.wait_ge(s_dg.h, init_dg)
                    e.tensor_scalar_mul(sdp[:, 896:1024], sdg[:], slopes[h]
                                        ).then_inc(s_dg.h, 1)
                    for sb in SBORD:
                        for kind, lo, hi in _subtiles(sb):
                            if kind != "diag":
                                continue
                            lh = lo // 1024
                            c0 = lo - lh * 1024
                            e.wait_ge(s_qk.h, ex_idx(h, sb, lh))
                            e.tensor_tensor(Sp[lh][:],
                                            Sp[lh][:],
                                            sdp[:, 896 - c0:1920 - c0],
                                            ALU.subtract).then_inc(s_dg.h, 1)
                    e.wait_ge(s_av.h, (h + 1) * NSB * 2)
                    if h >= 1:
                        e.wait_ge(s_dn.h, 16 * h)
                    e.tensor_copy(QT[(h % 2) * 64:(h % 2) * 64 + 64,
                                     (h // 2) * TOWN:(h // 2 + 1) * TOWN],
                                  Gp[0:64, :])
                    e.tensor_copy(dnall[64:65, :], Gp[64:65, :]).then_inc(s_gd.h, 1)
                ph.on("vector", f_v)
                s_dg.v += 1
                for sb in range(NSB):
                    for kind, lo, hi in _subtiles(sb):
                        if kind == "diag":
                            s_dg.v += 1
                s_gd.v += 1

                def f_g2(e, h=h):
                    e.wait_ge(s_gd.h, h + 1)
                    e.dma_start(out=dnall[32 + h:33 + h, :], in_=dnall[64:65, :]
                                ).then_inc(s_dn.h, 16)
                ph.on("gpsimd", f_g2)
                s_dn.v += 16

                def f_sc(e, h=h):
                    npc = 0
                    for sb in SBORD:
                        diag_halves = {lo // 1024 for kind, lo, hi in _subtiles(sb)
                                       if kind == "diag"}
                        for lh in range(2):
                            idx = ex_idx(h, sb, lh)
                            e.wait_ge(s_qk.h, idx)
                            if lh in diag_halves:
                                e.wait_ge(s_dg.h, _dgc[(h, sb, lh)])
                            if idx > 4:
                                e.wait_ge(s_av.h, idx - 4)
                            e.activation(Abuf[sb & 1][lh][:], Sp[lh][:], AF.Exp
                                         ).then_inc(s_ex.h, 1)
                ph.on("scalar", f_sc)
                s_ex.v += NSB * 2
            ph.emit()

        if dbg and upto == 3:
            _dump(nc, ctx, [(qtd, QT)])
        if upto == 3:
            return nc

        with ExitStack() as p4:
            WoT = p4.enter_context(nc.sbuf_tensor("WoT", [128, 8192], F32))
            bob = p4.enter_context(nc.sbuf_tensor("bob", [128, 1024], F32))
            sbo = p4.enter_context(nc.sbuf_tensor("sbo", [1, 1024], F32))
            rtmp = p4.enter_context(nc.sbuf_tensor("rtmp", [1, 2 * TOWN], F32))
            xres = p4.enter_context(nc.sbuf_tensor("xres", [128, 2048], F32))
            ya = p4.enter_context(nc.sbuf_tensor("ya", [128, 1024], F32))
            yb = p4.enter_context(nc.sbuf_tensor("yb", [128, 1024], F32))
            xc4 = p4.enter_context(nc.sbuf_tensor("xc4", [128, 1024], F32))
            x2t = p4.enter_context(nc.sbuf_tensor("x2t", [128, 2048], F32))
            mn4 = p4.enter_context(nc.sbuf_tensor("mn4", [128, 1], F32))
            vr4 = p4.enter_context(nc.sbuf_tensor("vr4", [128, 1], F32))
            sd4 = p4.enter_context(nc.sbuf_tensor("sd4", [128, 1], F32))
            iv4 = p4.enter_context(nc.sbuf_tensor("iv4", [128, 1], F32))

            s4lw = Sem(ctx, nc, "s4lw")
            s4tr = Sem(ctx, nc, "s4tr")
            s4te = Sem(ctx, nc, "s4te")
            s4rg = Sem(ctx, nc, "s4rg")
            s4rv = Sem(ctx, nc, "s4rv")
            s4bc = Sem(ctx, nc, "s4bc")
            s4nm = Sem(ctx, nc, "s4nm")
            s4x = Sem(ctx, nc, "s4x")
            s4m = Sem(ctx, nc, "s4m")
            s4e = Sem(ctx, nc, "s4e")
            s4s = Sem(ctx, nc, "s4s")
            s4w = Sem(ctx, nc, "s4w")

            with nc.psum_tensor("pbc4", [128, 2048], F32) as pbc4, \
                 nc.psum_tensor("pbc4b", [128, 2048], F32) as pbc4b:
                pbc = [pbc4, pbc4b]
                ph = Phase(nc)

                def f_sync(e):
                    e.dma_start(out=sbo[:], in_=bo[:]).then_inc(s4lw.h, 16)
                    for ci in range(8):
                        e.dma_start(out=WoT[:, ci * 1024:(ci + 1) * 1024],
                                    in_=wot[ci * 128:(ci + 1) * 128, :]).then_inc(s4lw.h, 16)
                ph.on("sync", f_sync)
                s4lw.v = 16 * 9

                def f_v0(e):
                    e.wait_ge(s_dn.h, 16 * H)
                    e.reciprocal(dnall[0:16, :], dnall[32:48, :]).then_inc(s4rv.h, 1)
                ph.on("vector", f_v0)

                def f_s0(e):
                    e.wait_ge(s4rv.h, 1)
                    e.activation(mn4[0:16, :], dnall[0:16, 0:1], AF.Identity
                                 ).then_inc(s4rv.h, 1)
                ph.on("scalar", f_s0)
                s4rv.v = 2

                def f_gp(e):
                    e.wait_ge(s4rv.h, 2)
                    for h in range(H):
                        if h >= 2:
                            e.wait_ge(s4bc.h, 2 + h - 1)
                        e.dma_start(out=rtmp[:, (h & 1) * TOWN:((h & 1) + 1) * TOWN],
                                    in_=dnall[h:h + 1, :]).then_inc(s4rg.h, 16)
                ph.on("gpsimd", f_gp)
                s4rg.v = 16 * H

                def f_t(e):
                    e.wait_ge(s4lw.h, 16)
                    for j in range(2):
                        e.matmul(pbc[0][:, j * 512:(j + 1) * 512], ones1[0:1, 0:128],
                                 sbo[0:1, j * 512:(j + 1) * 512], start=True, stop=True
                                 ).then_inc(s4bc.h, 1)
                    for h in range(H):
                        e.wait_ge(s4rg.h, 16 * (h + 1))
                        if h == 0:
                            e.wait_ge(s4nm.h, 1)
                        if h >= 2:
                            e.wait_ge(s4nm.h, h)
                        for j in range(4):
                            ins = e.matmul(pbc[h & 1][:, j * 512:(j + 1) * 512],
                                           ones1[0:1, 0:128],
                                           rtmp[0:1, (h & 1) * TOWN + j * 512:(h & 1) * TOWN + (j + 1) * 512],
                                           start=True, stop=True)
                            if j == 3:
                                ins.then_inc(s4bc.h, 1)
                ph.on("tensor", f_t)
                s4bc.v = 2 + H

                def f_v(e):
                    e.wait_ge(s4bc.h, 2)
                    e.tensor_copy(bob[:], pbc[0][:, 0:1024]).then_inc(s4nm.h, 1)
                    for h in range(H):
                        e.wait_ge(s4bc.h, 2 + h + 1)
                        e.tensor_tensor(QT[(h % 2) * 64:(h % 2) * 64 + 64,
                                           (h // 2) * TOWN:(h // 2 + 1) * TOWN],
                                        QT[(h % 2) * 64:(h % 2) * 64 + 64,
                                           (h // 2) * TOWN:(h // 2 + 1) * TOWN],
                                        pbc[h & 1][(h % 2) * 64:(h % 2) * 64 + 64, :],
                                        ALU.mult).then_inc(s4nm.h, 1)
                ph.on("vector", f_v)
                s4nm.v = 1 + H
                ph.emit()

            with nc.psum_tensor("Wop0", [128, 1024], F32) as Wop0, \
                 nc.psum_tensor("Wop1", [128, 1024], F32) as Wop1:
                Wop = [Wop0, Wop1]
                ph = Phase(nc)

                def f_sync(e):
                    for tb in range(NTB):
                        if tb >= 2:
                            e.wait_ge(s4e.h, tb - 1)
                        e.dma_start(out=xres[:, (tb & 1) * 1024:((tb & 1) + 1) * 1024],
                                    in_=xf[tb * 128:(tb + 1) * 128, :]).then_inc(s4x.h, 16)
                        if tb >= 1:
                            e.wait_ge(s4e.h, tb)
                            e.dma_start(out=x2s[(tb - 1) * 128:tb * 128, :],
                                        in_=x2t[:, ((tb - 1) & 1) * 1024:(((tb - 1) & 1) + 1) * 1024]
                                        ).then_inc(s4w.h, 16)
                    e.wait_ge(s4e.h, NTB)
                    e.dma_start(out=x2s[(NTB - 1) * 128:NTB * 128, :],
                                in_=x2t[:, ((NTB - 1) & 1) * 1024:(((NTB - 1) & 1) + 1) * 1024]
                                ).then_inc(s4w.h, 16)
                    e.wait_ge(s4w.h, 16 * NTB)
                ph.on("sync", f_sync)
                s4x.v = 16 * NTB
                s4w.v = 16 * NTB

                def f_t(e):
                    e.wait_ge(s4lw.h, 16 * 9)
                    for tb in range(NTB):
                        if tb >= 2:
                            e.wait_ge(s4e.h, tb - 1)
                        for hh in range(2):
                            for ci in range(8):
                                ins = e.matmul(
                                    Wop[tb & 1][:, hh * 512:(hh + 1) * 512],
                                    QT[:, ci * TOWN + tb * 128:ci * TOWN + (tb + 1) * 128],
                                    WoT[:, ci * 1024 + hh * 512:ci * 1024 + (hh + 1) * 512],
                                    start=(ci == 0), stop=(ci == 7))
                                if hh == 1 and ci == 7:
                                    ins.then_inc(s4m.h, 1)
                ph.on("tensor", f_t)
                s4m.v = NTB

                def f_v(e):
                    for tb in range(NTB):
                        sl = tb & 1
                        e.wait_ge(s4m.h, tb + 1)
                        e.wait_ge(s4x.h, 16 * (tb + 1))
                        if tb >= 2:
                            e.wait_ge(s4w.h, 16 * (tb - 1))
                        e.tensor_tensor(ya[:], Wop[sl][:], bob[:], ALU.add)
                        e.tensor_tensor(yb[:], ya[:],
                                        xres[:, sl * 1024:(sl + 1) * 1024], ALU.add)
                        e.reduce_sum(mn4[:], yb[:], AXX).then_inc(s4s.h, 1)
                        e.wait_ge(s4s.h, 6 * tb + 2)
                        e.tensor_scalar(xc4[:], yb[:], vr4[:], None, ALU.subtract)
                        e.tensor_tensor(ya[:], xc4[:], xc4[:], ALU.mult)
                        e.reduce_sum(mn4[:], ya[:], AXX).then_inc(s4s.h, 1)
                        e.wait_ge(s4s.h, 6 * tb + 4)
                        e.reciprocal(iv4[:], sd4[:]).then_inc(s4s.h, 1)
                        e.wait_ge(s4s.h, 6 * tb + 6)
                        e.tensor_scalar(x2t[:, sl * 1024:(sl + 1) * 1024], xc4[:],
                                        vr4[:], None, ALU.mult)
                        e.tensor_reduce(amaxc[:, tb:tb + 1],
                                        x2t[:, sl * 1024:(sl + 1) * 1024], AXX,
                                        ALU.max, apply_absolute_value=True
                                        ).then_inc(s4e.h, 1)
                ph.on("vector", f_v)
                s4e.v = NTB

                def f_sc(e):
                    for tb in range(NTB):
                        e.wait_ge(s4s.h, 6 * tb + 1)
                        e.activation(vr4[:], mn4[:], AF.Identity, scale=1.0 / C
                                     ).then_inc(s4s.h, 1)
                        e.wait_ge(s4s.h, 6 * tb + 3)
                        e.activation(sd4[:], mn4[:], AF.Sqrt, bias=sEps[:],
                                     scale=1.0 / C).then_inc(s4s.h, 1)
                        e.wait_ge(s4s.h, 6 * tb + 5)
                        e.activation(vr4[:], iv4[:], AF.Identity).then_inc(s4s.h, 1)
                ph.on("scalar", f_sc)
                s4s.v = 6 * NTB
                ph.emit()

        if dbg and upto == 4:
            _dump(nc, ctx, [(qtd, QT)])
        if upto == 4:
            return nc

        qts.close()

        with ExitStack() as p56:
            xqT = p56.enter_context(nc.sbuf_tensor("xqT", [128, 8 * TOWN], BF16))
            ccst = p56.enter_context(nc.sbuf_tensor("ccst", [128, 128], F32))
            csel = p56.enter_context(nc.sbuf_tensor("csel", [128, 1], F32))
            crow = p56.enter_context(nc.sbuf_tensor("crow", [1, 128], F32))
            g1 = p56.enter_context(nc.sbuf_tensor("g1", [1, 1], F32))
            x2q = p56.enter_context(nc.sbuf_tensor("x2q", [128, 2048], F32))
            qa = p56.enter_context(nc.sbuf_tensor("qa", [128, 1024], F32))
            qb = p56.enter_context(nc.sbuf_tensor("qb", [128, 1024], F32))
            xqt = p56.enter_context(nc.sbuf_tensor("xqt", [128, 2048], BF16))

            s5a = Sem(ctx, nc, "s5a")
            s5f = Sem(ctx, nc, "s5f")
            s5g = Sem(ctx, nc, "s5g")
            s4e_full = s4e.v
            s5l = Sem(ctx, nc, "s5l")
            s5c = Sem(ctx, nc, "s5c")
            s5b = Sem(ctx, nc, "s5b")
            s5t = Sem(ctx, nc, "s5t")
            s5x = Sem(ctx, nc, "s5x")
            s5q = Sem(ctx, nc, "s5q")
            s5z = Sem(ctx, nc, "s5z")

            cselb = p56.enter_context(nc.sbuf_tensor("cselb", [128, 1], F32))
            swfs6 = p56.enter_context(nc.sbuf_tensor("swfs6", [1, 1024], F32))
            sfb6 = p56.enter_context(nc.sbuf_tensor("sfb6", [1, 1024], F32))
            w8sb6 = p56.enter_context(nc.sbuf_tensor("w8sb6", [128, 8192], I8))
            w8bf_ = p56.enter_context(nc.sbuf_tensor("w8bf", [128, 8192], BF16))
            wfT_ = p56.enter_context(nc.sbuf_tensor("wfT", [128, 8192], BF16))
            s6l_ = Sem(ctx, nc, "s6l")
            s6c_ = Sem(ctx, nc, "s6c")
            s6t_ = Sem(ctx, nc, "s6t")
            sG = p56.enter_context(nc.sbuf_tensor("sG", [1, 1], F32))
            sScp = p56.enter_context(nc.sbuf_tensor("sScp", [1, 1], F32))
            psH = nc.psum_tensor("p5b", [128, 2], F32)
            p5b = psH.__enter__()
            if True:
                ph = Phase(nc)

                def f_s_fence(e):
                    e.wait_ge(s4e.h, s4e_full)
                    e.activation(ccst[:, 0:NTB], amaxc[:], AF.Identity
                                 ).then_inc(s5f.h, 1)
                ph.on("scalar", f_s_fence)
                s5f.v = 1

                def f_v_pre(e):
                    e.wait_ge(s5f.h, 1)
                    e.tensor_reduce(csel[:], amaxc[:], AXX, ALU.max).then_inc(s5a.h, 1)
                    e.wait_ge(s5a.h, 2)
                    e.memset(ccst[:], 0.0)
                    e.tensor_copy(ccst[:, 0:1], cselb[:]).then_inc(s5a.h, 1)
                ph.on("vector", f_v_pre)

                def f_s(e):
                    e.wait_ge(s5a.h, 1)
                    e.activation(cselb[:], csel[:], AF.Identity).then_inc(s5a.h, 1)
                    e.wait_ge(s5a.h, 4)
                    e.activation(cselb[:], csel[:], AF.Identity).then_inc(s5a.h, 1)
                    e.wait_ge(s5a.h, 6)
                    e.activation(sG[:], g1[:], AF.Identity, scale=1.0 / 127.0
                                 ).then_inc(s5a.h, 1)
                    e.wait_ge(s5a.h, 8)
                    e.activation(sScp[:], sS[:], AF.Identity).then_inc(s5a.h, 1)
                    e.wait_ge(s5b.h, 1)
                    e.copy(sRb[:], p5b[:, 0:1])
                    e.copy(sSb[:], p5b[:, 1:2]).then_inc(s5a.h, 1)
                ph.on("scalar", f_s)

                def f_sync(e):
                    e.dma_start(out=swfs6[:], in_=wfs[:]).then_inc(s6l_.h, 16)
                    e.dma_start(out=sfb6[:], in_=fbi[:]).then_inc(s6l_.h, 16)
                    for fj in range(8):
                        e.dma_start(out=w8sb6[:, fj * 1024:(fj + 1) * 1024],
                                    in_=wf8[fj * 128:(fj + 1) * 128, :]
                                    ).then_inc(s6l_.h, 16)
                    e.wait_ge(s5a.h, 3)
                    e.dma_start(out=ccin[:], in_=ccst[:]).then_inc(s5l.h, 16)
                ph.on("sync", f_sync)

                def f_gp(e):
                    e.wait_ge(s5l.h, 16)
                    e.collective_compute(
                        "AllReduce", ALU.max,
                        replica_groups=[[0, 1, 2, 3, 4, 5, 6, 7]],
                        ins=[ccin.ap().opt()], outs=[ccout.ap().opt()],
                    ).then_inc(s5c.h, 1)
                    e.wait_ge(s5c.h, 1)
                    e.dma_start(out=ccst[:], in_=ccout[:]).then_inc(s5l.h, 16)
                    e.wait_ge(s5g.h, 1)
                    e.dma_start(out=gtmp[:], in_=ccst[:, 20:21]).then_inc(s5l.h, 16)
                    e.wait_ge(s5l.h, 48)
                    e.dma_start(out=crow[:], in_=gtmp[:]).then_inc(s5l.h, 16)
                ph.on("gpsimd", f_gp)
                s5c.v = 1
                s5l.v = 64

                def f_t(e):
                    e.wait_ge(s5a.h, 10)
                    e.matmul(p5b[:, 0:1], ones1[0:1, 0:128], sRi[0:1, 0:1],
                             start=True, stop=True)
                    e.matmul(p5b[:, 1:2], ones1[0:1, 0:128], sS[0:1, 0:1],
                             start=True, stop=True).then_inc(s5b.h, 1)
                ph.on("tensor", f_t)
                s5b.v = 1
                s5a.v = 11

            def f_conv_early(e):
                e.wait_ge(s6l_.h, 16 * 10)
                for fj in range(8):
                    e.tensor_copy(w8bf_[:, fj * 1024:(fj + 1) * 1024],
                                  w8sb6[:, fj * 1024:(fj + 1) * 1024]).then_inc(s6c_.h, 1)
            ph.on("vector", f_conv_early)

            def f_v_post(e):
                e.wait_ge(s5l.h, 32)
                e.tensor_reduce(csel[:], ccst[:], AXX, ALU.max).then_inc(s5a.h, 1)
                e.wait_ge(s5a.h, 5)
                e.tensor_copy(ccst[:, 20:21], cselb[:]).then_inc(s5g.h, 1)
                e.wait_ge(s5l.h, 64)
                e.tensor_reduce(g1[:], crow[:], AXX, ALU.max).then_inc(s5a.h, 1)
                e.wait_ge(s5a.h, 7)
                e.tensor_scalar(sS[:], sG[:], 1.0, EPS, ALU.mult, ALU.max
                                ).then_inc(s5a.h, 1)
                e.wait_ge(s5a.h, 9)
                e.reciprocal(sRi[:], sScp[:]).then_inc(s5a.h, 1)
            ph.on("vector", f_v_post)

            def f_trans_early(e):
                for fj in range(8):
                    e.wait_ge(s6c_.h, fj + 1)
                    for ci in range(8):
                        e.dma_start_transpose(
                            wfT_[:, ci * 1024 + fj * 128:ci * 1024 + (fj + 1) * 128],
                            w8bf_[:, fj * 1024 + ci * 128:fj * 1024 + (ci + 1) * 128]
                        ).then_inc(s6t_.h, 16)
            ph.on("sync", f_trans_early)
            s6l_.v = 16 * 10
            s6c_.v = 8
            s6t_.v = 16 * 64


            def f_sync(e):
                def trans(tb):
                    e.wait_ge(s5q.h, tb + 1)
                    for ci in range(8):
                        e.dma_start_transpose(
                            xqT[:, ci * TOWN + tb * 128:ci * TOWN + (tb + 1) * 128],
                            xqt[:, (tb & 1) * 1024 + ci * 128:(tb & 1) * 1024 + (ci + 1) * 128]
                        ).then_inc(s5t.h, 16)
                for tb in range(NTB):
                    if tb >= 2:
                        e.wait_ge(s5q.h, tb - 1)
                    e.dma_start(out=x2q[:, (tb & 1) * 1024:((tb & 1) + 1) * 1024],
                                in_=x2s[tb * 128:(tb + 1) * 128, :]).then_inc(s5x.h, 16)
                    if tb >= 1:
                        trans(tb - 1)
                trans(NTB - 1)
            ph.on("sync", f_sync)
            s5x.v = 16 * NTB
            s5t.v = 16 * 8 * NTB

            def f_v(e):
                e.wait_ge(s5a.h, 11)
                for tb in range(NTB):
                    sl = tb & 1
                    e.wait_ge(s5x.h, 16 * (tb + 1))
                    if tb >= 2:
                        e.wait_ge(s5t.h, 16 * 8 * (tb - 1))
                    e.tensor_scalar(qa[:], x2q[:, sl * 1024:(sl + 1) * 1024],
                                    sRb[:], C3, ALU.mult, ALU.add)
                    e.tensor_scalar(qb[:], qa[:], C3, -128.0, ALU.subtract, ALU.max)
                    e.tensor_scalar(xqt[:, sl * 1024:(sl + 1) * 1024], qb[:],
                                    127.0, None, ALU.min).then_inc(s5q.h, 1)
            ph.on("vector", f_v)
            s5q.v = NTB

            if dbg:
                with nc.sbuf_tensor("dbgf", [128, 8 * TOWN], F32) as dbgf:
                    sdx = Sem(ctx, nc, "sdx")
                    ph = Phase(nc)

                    def f_dv(e):
                        for j in range(16):
                            e.tensor_copy(dbgf[:, j * 1024:(j + 1) * 1024],
                                          xqT[:, j * 1024:(j + 1) * 1024])
                        e.tensor_copy(dbgf[0:1, 0:1], sS[:])
                        e.tensor_copy(dbgf[0:1, 1:2], sRi[:]).then_inc(sdx.h, 1)
                    ph.on("vector", f_dv)

                    def f_ds(e):
                        e.wait_ge(sdx.h, 1)
                        e.dma_start(out=qtd[:], in_=dbgf[:]).then_inc(sdx.h, 16)
                        e.wait_ge(sdx.h, 17)
                    ph.on("sync", f_ds)
                    ph.emit()

            with ExitStack() as p6:
                w8bf = w8bf_
                wfT = wfT_
                cb6 = p6.enter_context(nc.sbuf_tensor("cb6", [128, 1024], F32))
                fb6 = p6.enter_context(nc.sbuf_tensor("fb6", [128, 1024], F32))
                x2r = p6.enter_context(nc.sbuf_tensor("x2r", [128, 2048], F32))
                ya6 = p6.enter_context(nc.sbuf_tensor("ya6", [128, 1024], F32))
                yb6 = p6.enter_context(nc.sbuf_tensor("yb6", [128, 1024], F32))
                xc6 = p6.enter_context(nc.sbuf_tensor("xc6", [128, 1024], F32))
                ot6 = p6.enter_context(nc.sbuf_tensor("ot6", [128, 2048], F32))
                mn6 = p6.enter_context(nc.sbuf_tensor("mn6", [128, 1], F32))
                vr6 = p6.enter_context(nc.sbuf_tensor("vr6", [128, 1], F32))
                sd6 = p6.enter_context(nc.sbuf_tensor("sd6", [128, 1], F32))
                iv6 = p6.enter_context(nc.sbuf_tensor("iv6", [128, 1], F32))

                s6l, s6c, s6t = s6l_, s6c_, s6t_
                s6b = Sem(ctx, nc, "s6b")
                s6g = Sem(ctx, nc, "s6g")
                s6x = Sem(ctx, nc, "s6x")
                s6m = Sem(ctx, nc, "s6m")
                s6e = Sem(ctx, nc, "s6e")
                s6s = Sem(ctx, nc, "s6s")
                s6o = Sem(ctx, nc, "s6o")

                with nc.psum_tensor("p6b", [128, 1024], F32) as p6bb, \
                     nc.psum_tensor("Fp0", [128, 1024], F32) as Fp0, \
                     nc.psum_tensor("Fp1", [128, 1024], F32) as Fp1:
                    Fp = [Fp0, Fp1]

                    def f_sync(e):
                        def store(tb):
                            e.wait_ge(s6e.h, tb + 1)
                            e.dma_start(out=out[tb * 128:(tb + 1) * 128, :],
                                        in_=ot6[:, (tb & 1) * 1024:((tb & 1) + 1) * 1024]
                                        ).then_inc(s6o.h, 16)
                        for tb in range(NTB):
                            if tb >= 2:
                                e.wait_ge(s6e.h, tb - 1)
                            e.dma_start(out=x2r[:, (tb & 1) * 1024:((tb & 1) + 1) * 1024],
                                        in_=x2s[tb * 128:(tb + 1) * 128, :]
                                        ).then_inc(s6x.h, 16)
                            if tb >= 1:
                                store(tb - 1)
                        store(NTB - 1)
                        e.wait_ge(s6o.h, 16 * NTB)
                    ph.on("sync", f_sync)
                    s6x.v = 16 * NTB
                    s6o.v = 16 * NTB

                    def f_t(e):
                        e.wait_ge(s6l.h, 32)
                        for j in range(2):
                            e.matmul(p6bb[:, j * 512:(j + 1) * 512], ones1[0:1, 0:128],
                                     swfs6[0:1, j * 512:(j + 1) * 512],
                                     start=True, stop=True).then_inc(s6b.h, 1)
                        e.wait_ge(s6g.h, 1)
                        for j in range(2):
                            e.matmul(p6bb[:, j * 512:(j + 1) * 512], ones1[0:1, 0:128],
                                     sfb6[0:1, j * 512:(j + 1) * 512],
                                     start=True, stop=True).then_inc(s6b.h, 1)
                        e.wait_ge(s6t.h, 16 * 64)
                        e.wait_ge(s5t.h, 16 * 8 * NTB)
                        for tb in range(NTB):
                            if tb >= 2:
                                e.wait_ge(s6e.h, tb - 1)
                            for hh in range(2):
                                for ci in range(8):
                                    ins = e.matmul(
                                        Fp[tb & 1][:, hh * 512:(hh + 1) * 512],
                                        xqT[:, ci * TOWN + tb * 128:ci * TOWN + (tb + 1) * 128],
                                        wfT[:, ci * 1024 + hh * 512:ci * 1024 + (hh + 1) * 512],
                                        start=(ci == 0), stop=(ci == 7))
                                    if hh == 1 and ci == 7:
                                        ins.then_inc(s6m.h, 1)
                    ph.on("tensor", f_t)
                    s6b.v = 4
                    s6m.v = NTB

                    def f_v(e):
                        e.wait_ge(s6b.h, 2)
                        e.tensor_scalar(cb6[:], p6bb[:], sSb[:], None, ALU.mult
                                        ).then_inc(s6g.h, 1)
                        e.wait_ge(s6b.h, 4)
                        e.tensor_copy(fb6[:], p6bb[:])
                        for tb in range(NTB):
                            sl = tb & 1
                            e.wait_ge(s6m.h, tb + 1)
                            e.wait_ge(s6x.h, 16 * (tb + 1))
                            if tb >= 2:
                                e.wait_ge(s6o.h, 16 * (tb - 1))
                            e.tensor_tensor(ya6[:], Fp[sl][:], cb6[:], ALU.mult)
                            e.tensor_tensor(yb6[:], ya6[:], fb6[:], ALU.add)
                            e.tensor_tensor(ya6[:], yb6[:],
                                            x2r[:, sl * 1024:(sl + 1) * 1024], ALU.add)
                            e.reduce_sum(mn6[:], ya6[:], AXX).then_inc(s6s.h, 1)
                            e.wait_ge(s6s.h, 6 * tb + 2)
                            e.tensor_scalar(xc6[:], ya6[:], vr6[:], None, ALU.subtract)
                            e.tensor_tensor(yb6[:], xc6[:], xc6[:], ALU.mult)
                            e.reduce_sum(mn6[:], yb6[:], AXX).then_inc(s6s.h, 1)
                            e.wait_ge(s6s.h, 6 * tb + 4)
                            e.reciprocal(iv6[:], sd6[:]).then_inc(s6s.h, 1)
                            e.wait_ge(s6s.h, 6 * tb + 6)
                            e.tensor_scalar(ot6[:, sl * 1024:(sl + 1) * 1024], xc6[:],
                                            vr6[:], None, ALU.mult).then_inc(s6e.h, 1)
                    ph.on("vector", f_v)
                    s6g.v = 1
                    s6e.v = NTB

                    def f_sc(e):
                        for tb in range(NTB):
                            e.wait_ge(s6s.h, 6 * tb + 1)
                            e.activation(vr6[:], mn6[:], AF.Identity, scale=1.0 / C
                                         ).then_inc(s6s.h, 1)
                            e.wait_ge(s6s.h, 6 * tb + 3)
                            e.activation(sd6[:], mn6[:], AF.Sqrt, bias=sEps[:],
                                         scale=1.0 / C).then_inc(s6s.h, 1)
                            e.wait_ge(s6s.h, 6 * tb + 5)
                            e.activation(vr6[:], iv6[:], AF.Identity).then_inc(s6s.h, 1)
                    ph.on("scalar", f_sc)
                    s6s.v = 6 * NTB
                    ph.emit()
            psH.__exit__(None, None, None)
        return nc
    return nc


def _dump(nc, ctx, pairs):
    sdm = Sem(ctx, nc, f"sdm{len(pairs)}")
    ph = Phase(nc)

    def f(e):
        n = 0
        for dst, src in pairs:
            e.dma_start(out=dst[:], in_=src[:]).then_inc(sdm.h, 16)
            n += 16
        e.wait_ge(sdm.h, n)
    ph.on("sync", f)
    ph.emit()



def _host_constants():
    slopes = _slopes()
    posv = np.arange(L, dtype=np.float32)[None, :]
    i = np.arange(128, dtype=np.float32)
    dgv = np.abs(i[:, None] - i[None, :]).astype(np.float32)
    coefv = np.zeros((32, 1), dtype=np.float32)
    for h in range(H):
        coefv[2 * h + 0, 0] = slopes[h]
        coefv[2 * h + 1, 0] = 2.0 * slopes[h]
    return posv, dgv, coefv


def make_in_maps(x, Wq, Wk, Wv, Wo, bov, W8, wsc, fbv):
    posv, dgv, coefv = _host_constants()
    wqT = np.ascontiguousarray(Wq.T)
    wkT = np.ascontiguousarray(Wk.T)
    wvT = np.ascontiguousarray(Wv.T)
    woT = np.ascontiguousarray(Wo.T)
    in_maps = []
    for c in range(NCORE):
        b, half = divmod(c, 2)
        xb = x[b] if half == 0 else x[b][::-1]
        xb = np.ascontiguousarray(xb)
        in_maps.append({
            "xf": xb, "xft": np.ascontiguousarray(xb.T),
            "wqt": wqT, "wkt": wkT, "wvt": wvT, "wot": woT,
            "bo": bov, "wf8": W8, "wfs": wsc, "fbi": fbv,
            "pos": posv, "dgi": dgv, "coef": coefv,
        })
    return in_maps


def kernel(L0_attn_q, L0_attn_Wq, L0_attn_Wk, L0_attn_Wv, L0_attn_Wo, L0_attn_bo,
           L0_ffn_weight_int8, L0_ffn_weight_scale, L0_ffn_bias):
    x = np.ascontiguousarray(np.asarray(L0_attn_q, dtype=np.float32))
    Wq = np.ascontiguousarray(np.asarray(L0_attn_Wq, dtype=np.float32))
    Wk = np.ascontiguousarray(np.asarray(L0_attn_Wk, dtype=np.float32))
    Wv = np.ascontiguousarray(np.asarray(L0_attn_Wv, dtype=np.float32))
    Wo = np.ascontiguousarray(np.asarray(L0_attn_Wo, dtype=np.float32))
    bov = np.ascontiguousarray(np.asarray(L0_attn_bo, dtype=np.float32).reshape(1, C))
    W8 = np.ascontiguousarray(np.asarray(L0_ffn_weight_int8, dtype=np.int8))
    wsc = np.ascontiguousarray(np.asarray(L0_ffn_weight_scale, dtype=np.float32).reshape(1, C))
    fbv = np.ascontiguousarray(np.asarray(L0_ffn_bias, dtype=np.float32).reshape(1, C))

    in_maps = make_in_maps(x, Wq, Wk, Wv, Wo, bov, W8, wsc, fbv)
    nc = build_nc()
    res = run_bass_kernel_spmd(nc, in_maps, core_ids=list(range(NCORE)))
    outv = np.empty((B, L, C), dtype=np.float32)
    for c in range(NCORE):
        b, half = divmod(c, 2)
        o = np.asarray(res.results[c]["out"])
        if half == 0:
            outv[b, 0:TOWN] = o
        else:
            outv[b, TOWN:L] = o[::-1]
    return outv


# version 38
# speedup vs baseline: 1.0099x; 1.0099x over previous
import numpy as np
from contextlib import ExitStack

import concourse.bass as bass
import concourse.mybir as mybir
from concourse.bass_utils import run_bass_kernel_spmd

F32 = mybir.dt.float32
BF16 = mybir.dt.bfloat16
I8 = mybir.dt.int8
AF = mybir.ActivationFunctionType
ALU = mybir.AluOpType
AXX = mybir.AxisListType.X

B, L, C, H, D = 4, 4096, 1024, 16, 64
TOWN = 2048
NCORE = 8
C3 = 12582912.0
EPS = 1e-5
NTB = TOWN // 128
NSB = L // 128


def _slopes():
    n = H
    start = 2.0 ** (-(2.0 ** (-(np.log2(n) - 3))))
    return np.array([start * start ** i for i in range(n)], dtype=np.float32)


def _subtiles(sb):
    s0 = sb * 128
    pieces = []
    if s0 >= TOWN:
        pieces.append(("up", 0, TOWN))
    else:
        if s0 > 0:
            pieces.append(("up", 0, s0))
        pieces.append(("diag", s0, s0 + 128))
        if s0 + 128 < TOWN:
            pieces.append(("low", s0 + 128, TOWN))
    res = []
    for kind, lo, hi in pieces:
        a = lo
        while a < hi:
            b2 = min(hi, (a // 512 + 1) * 512)
            res.append((kind, a, b2))
            a = b2
    return res


class Sem:
    def __init__(self, ctx, nc, name):
        self.h = ctx.enter_context(nc.semaphore(name))
        self.v = 0


class Phase:
    ENGS = ("sync", "tensor", "vector", "scalar", "gpsimd")

    def __init__(self, nc):
        self.nc = nc
        self.q = {e: [] for e in self.ENGS}

    def on(self, eng, fn):
        self.q[eng].append(fn)

    def emit(self):
        with self.nc.Block() as blk:
            for e in self.ENGS:
                if self.q[e]:
                    getattr(blk, e)(lambda eng, _f=tuple(self.q[e]): [f(eng) for f in _f])


def build_nc(upto=6, dbg=False):
    slopes = [float(s) for s in _slopes()]
    nc = bass.Bass("TRN2", target_bir_lowering=False)

    xf = nc.dram_tensor("xf", [L, C], F32, kind="ExternalInput")
    xft = nc.dram_tensor("xft", [C, L], F32, kind="ExternalInput")
    wqt = nc.dram_tensor("wqt", [C, C], F32, kind="ExternalInput")
    wkt = nc.dram_tensor("wkt", [C, C], F32, kind="ExternalInput")
    wvt = nc.dram_tensor("wvt", [C, C], F32, kind="ExternalInput")
    wot = nc.dram_tensor("wot", [C, C], F32, kind="ExternalInput")
    bo = nc.dram_tensor("bo", [1, C], F32, kind="ExternalInput")
    wf8 = nc.dram_tensor("wf8", [C, C], I8, kind="ExternalInput")
    wfs = nc.dram_tensor("wfs", [1, C], F32, kind="ExternalInput")
    fbi = nc.dram_tensor("fbi", [1, C], F32, kind="ExternalInput")
    pos = nc.dram_tensor("pos", [1, L], F32, kind="ExternalInput")
    dgi = nc.dram_tensor("dgi", [128, 128], F32, kind="ExternalInput")
    coef = nc.dram_tensor("coef", [32, 1], F32, kind="ExternalInput")
    out = nc.dram_tensor("out", [TOWN, C], F32, kind="ExternalOutput")

    spill = "ExternalOutput" if dbg else "Internal"
    ktsH = nc.dram_tensor("ktsH", [C, L], BF16, kind=spill)
    ktsL = nc.dram_tensor("ktsL", [C, L], BF16, kind=spill)
    vsp = nc.dram_tensor("vsp", [L, C], F32, kind=spill)
    x2s = nc.dram_tensor("x2s", [TOWN, C], F32, kind=spill)
    ccin = nc.dram_tensor("ccin", [128, 128], F32, kind="Internal")
    ccout = nc.dram_tensor("ccout", [128, 128], F32, kind="Internal")
    gtmp = nc.dram_tensor("gtmp", [128], F32, kind="Internal")
    recd = nc.dram_tensor("recd", [16, TOWN], F32, kind="Internal")
    qtd = nc.dram_tensor("qtd", [128, 8 * TOWN], F32, kind="ExternalOutput") if dbg else None
    rcd = nc.dram_tensor("rcd", [16, TOWN], F32, kind="ExternalOutput") if dbg else None

    ctx = ExitStack()
    with ctx:
        ones1 = ctx.enter_context(nc.sbuf_tensor("ones1", [1, 128], F32))
        sS = ctx.enter_context(nc.sbuf_tensor("sS", [1, 1], F32))
        sRi = ctx.enter_context(nc.sbuf_tensor("sRi", [1, 1], F32))
        sRb = ctx.enter_context(nc.sbuf_tensor("sRb", [128, 1], F32))
        sSb = ctx.enter_context(nc.sbuf_tensor("sSb", [128, 1], F32))
        amaxc = ctx.enter_context(nc.sbuf_tensor("amaxc", [128, NTB], F32))
        sEps = ctx.enter_context(nc.sbuf_tensor("sEps", [128, 1], F32))
        qts = ctx.enter_context(ExitStack())
        QT = qts.enter_context(nc.sbuf_tensor("QT", [128, 8 * TOWN], F32))
        sdg = qts.enter_context(nc.sbuf_tensor("sdg", [128, 128], F32))
        dnall = qts.enter_context(nc.sbuf_tensor("dnall", [128, TOWN], F32))
        AUGT = qts.enter_context(nc.sbuf_tensor("AUGT", [96, L], BF16))
        AUG1, AUG2, AUG3 = AUGT[0:32], AUGT[32:64], AUGT[64:96]
        onesb2 = qts.enter_context(nc.sbuf_tensor("onesb2", [33, TOWN], BF16))
        onesb = onesb2[0:1]
        negb = onesb2[32:33]
        p0s = ExitStack()
        RAMP = p0s.enter_context(nc.sbuf_tensor("RAMP", [32, L], F32))
        spos = p0s.enter_context(nc.sbuf_tensor("spos", [1, L], F32))
        rs1 = p0s.enter_context(nc.sbuf_tensor("rs1", [32, L], F32))
        rs2 = p0s.enter_context(nc.sbuf_tensor("rs2", [32, L], F32))
        scoef = p0s.enter_context(nc.sbuf_tensor("scoef", [32, 1], F32))
        augt2 = p0s.enter_context(nc.sbuf_tensor("augt2", [32, L], BF16))
        augt3 = p0s.enter_context(nc.sbuf_tensor("augt3", [32, L], BF16))

        s0ld = Sem(ctx, nc, "s0ld")
        s0t = Sem(ctx, nc, "s0t")
        s0v = Sem(ctx, nc, "s0v")
        s0d = Sem(ctx, nc, "s0d")
        ph = Phase(nc)

        def _p0_sync(e):
            for dst, src in ((spos, pos), (sdg, dgi), (scoef, coef)):
                e.dma_start(out=dst[:], in_=src[:]).then_inc(s0ld.h, 16)
        ph.on("sync", _p0_sync)
        s0ld.v = 48

        def _p0_vec(e):
            e.memset(ones1[:], 1.0)
            e.memset(sEps[:], EPS)
        ph.on("vector", _p0_vec)
        ph.emit()

        with nc.psum_tensor("pbc0", [32, 2048], F32) as pbc0:
            ph = Phase(nc)

            def _p0b_t(e):
                e.wait_ge(s0ld.h, 48)
                for half in range(2):
                    if half == 1:
                        e.wait_ge(s0v.h, 1)
                    for j in range(4):
                        e.matmul(pbc0[:, j * 512:(j + 1) * 512], ones1[0:1, 0:32],
                                 spos[0:1, half * 2048 + j * 512:half * 2048 + (j + 1) * 512],
                                 start=True, stop=True).then_inc(s0t.h, 1)
            ph.on("tensor", _p0b_t)
            s0t.v = 8

            def _p0b_v(e):
                e.memset(onesb2[0:1, :], 1.0)
                e.memset(onesb2[32:33, :], -1.0)
                for half in range(2):
                    e.wait_ge(s0t.h, 4 * (half + 1))
                    e.tensor_scalar(RAMP[:, half * 2048:(half + 1) * 2048], pbc0[:],
                                    scoef[:], None, ALU.mult).then_inc(s0v.h, 1)
                e.tensor_copy(AUG1[:], RAMP[:])
                e.tensor_tensor(rs1[:], RAMP[:], AUG1[:], ALU.subtract)
                e.tensor_copy(augt2[:], rs1[:])
                e.tensor_tensor(rs2[:], rs1[:], augt2[:], ALU.subtract)
                e.tensor_copy(augt3[:], rs2[:]).then_inc(s0v.h, 1)
            ph.on("vector", _p0b_v)
            s0v.v = 3

            def _p0b_s(e):
                e.wait_ge(s0v.h, 3)
                e.dma_start(out=AUGT[32:64, :], in_=augt2[:]).then_inc(s0d.h, 16)
                e.dma_start(out=AUGT[64:96, :], in_=augt3[:]).then_inc(s0d.h, 16)
            ph.on("scalar", _p0b_s)
            s0d.v = 32
            ph.emit()
        p0s.close()

        with ExitStack() as p2:
            WATH = [p2.enter_context(nc.sbuf_tensor(f"WATH{i}", [128, 8192], BF16))
                    for i in range(2)]
            WATL = [p2.enter_context(nc.sbuf_tensor(f"WATL{i}", [128, 8192], BF16))
                    for i in range(2)]
            wldf = p2.enter_context(nc.sbuf_tensor("wldf", [128, 1024], F32))
            xTF = p2.enter_context(nc.sbuf_tensor("xTF", [128, 4096], F32))
            xTH = p2.enter_context(nc.sbuf_tensor("xTH", [128, 8192], BF16))
            xTL = p2.enter_context(nc.sbuf_tensor("xTL", [128, 8192], BF16))
            kstH = p2.enter_context(nc.sbuf_tensor("kstH", [128, 512], BF16))
            vst = p2.enter_context(nc.sbuf_tensor("vst", [128, 512], F32))
            kstL = p2.enter_context(nc.sbuf_tensor("kstL", [128, 512], BF16))
            Kp = [p2.enter_context(nc.psum_tensor(f"Kp{i}", [128, 512], F32)) for i in range(2)]
            Vp = [p2.enter_context(nc.psum_tensor(f"Vp{i}", [128, 1024], F32)) for i in range(2)]

            s_lw = Sem(ctx, nc, "s_lw")
            s_lx = Sem(ctx, nc, "s_lx")
            s_ws = Sem(ctx, nc, "s_ws")
            s_xs = Sem(ctx, nc, "s_xs")
            s_mk = Sem(ctx, nc, "s_mk")
            s_ek = Sem(ctx, nc, "s_ek")
            s_wk = Sem(ctx, nc, "s_wk")
            s_mv = Sem(ctx, nc, "s_mv")
            s_ev = Sem(ctx, nc, "s_ev")
            s_wv = Sem(ctx, nc, "s_wv")

            ph = Phase(nc)
            wsrc_map = {"wk": wkt, "wv": wvt, "wq": wqt}

            def sched_weight_load(wname, wbuf, after_mk, after_mv):
                wsrc = wsrc_map[wname]
                ws_base = s_ws.v

                def f_sync(e, wsrc=wsrc, after_mk=after_mk, after_mv=after_mv,
                           ws_base=ws_base, first=(s_lw.v == 0)):
                    if first:
                        e.wait_ge(s0d.h, 32)
                    for ci in range(8):
                        if ws_base + ci > 0:
                            e.wait_ge(s_ws.h, ws_base + ci)
                        e.dma_start(out=wldf[:],
                                    in_=wsrc[ci * 128:(ci + 1) * 128, :]).then_inc(s_lw.h, 16)
                ph.on("gpsimd", f_sync)
                lw_base = s_lw.v
                s_lw.v += 16 * 8

                def f_vsp(e, lw_base=lw_base, wbuf=wbuf,
                          after_mk=after_mk, after_mv=after_mv):
                    if after_mk:
                        e.wait_ge(s_mk.h, after_mk)
                    if after_mv:
                        e.wait_ge(s_mv.h, after_mv)
                    for ci in range(8):
                        e.wait_ge(s_lw.h, lw_base + 16 * (ci + 1))
                        e.tensor_copy(WATH[wbuf][:, ci * 1024:(ci + 1) * 1024],
                                      wldf[:])
                        e.tensor_tensor(WATL[wbuf][:, ci * 1024:(ci + 1) * 1024],
                                        wldf[:],
                                        WATH[wbuf][:, ci * 1024:(ci + 1) * 1024],
                                        ALU.subtract).then_inc(s_ws.h, 1)
                ph.on("vector", f_vsp)
                s_ws.v += 8

            def sched_tile(ti, tt, mode, wbuf):
                slot = ti & 1
                t0 = tt * 512
                mk_prev2 = max(0, s_mk.v - (8 if mode in ("k", "q") else 0) * 1)
                need_mk = tile_mk.get(ti - 2, 0)
                need_mv = tile_mv.get(ti - 2, 0)

                xs0 = s_xs.v
                for hf in range(2):
                    fslot = (2 * ti + hf) & 1
                    def f_sync(e, t0=t0, hf=hf, fslot=fslot,
                               xs_need=max(0, s_xs.v - 1)):
                        if xs_need:
                            e.wait_ge(s_xs.h, xs_need)
                        for ci4 in range(4):
                            ci = hf * 4 + ci4
                            e.dma_start(out=xTF[:, fslot * 2048 + ci4 * 512:
                                                fslot * 2048 + (ci4 + 1) * 512],
                                        in_=xft[ci * 128:(ci + 1) * 128, t0:t0 + 512]
                                        ).then_inc(s_lx.h, 16)
                    ph.on("scalar", f_sync)
                    s_lx.v += 16 * 4
                    lx_need = s_lx.v

                    def f_xsp(e, slot=slot, hf=hf, fslot=fslot, lx_need=lx_need,
                              need_mk=need_mk, need_mv=need_mv):
                        e.wait_ge(s_lx.h, lx_need)
                        if hf == 0:
                            if need_mk:
                                e.wait_ge(s_mk.h, need_mk)
                            if need_mv:
                                e.wait_ge(s_mv.h, need_mv)
                        o = slot * 4096 + hf * 2048
                        fo = fslot * 2048
                        e.tensor_copy(xTH[:, o:o + 2048], xTF[:, fo:fo + 2048])
                        e.tensor_tensor(xTL[:, o:o + 2048], xTF[:, fo:fo + 2048],
                                        xTH[:, o:o + 2048],
                                        ALU.subtract).then_inc(s_xs.h, 1)
                    ph.on("vector", f_xsp)
                    s_xs.v += 1
                xs_need2 = s_xs.v
                ws_need = s_ws.v

                if mode in ("k", "q"):
                    for fj in range(8):
                        kk = s_mk.v
                        pslot = kk & 1

                        def f_t(e, fj=fj, slot=slot, pslot=pslot, xs_need2=xs_need2,
                                ws_need=ws_need, ek_need=max(0, s_ek.v - 1)):
                            e.wait_ge(s_xs.h, xs_need2)
                            e.wait_ge(s_ws.h, ws_need)
                            if ek_need:
                                e.wait_ge(s_ek.h, ek_need)
                            np_ = 0
                            for ci in range(8):
                                for wa, xb in ((WATH[wbuf], xTH), (WATH[wbuf], xTL),
                                               (WATL[wbuf], xTH)):
                                    np_ += 1
                                    ins = e.matmul(
                                        Kp[pslot][:],
                                        wa[:, ci * 1024 + fj * 128:ci * 1024 + (fj + 1) * 128],
                                        xb[:, slot * 4096 + ci * 512:slot * 4096 + (ci + 1) * 512],
                                        start=(np_ == 1), stop=(np_ == 24))
                                    if np_ == 24:
                                        ins.then_inc(s_mk.h, 1)
                        ph.on("tensor", f_t)
                        s_mk.v += 1

                        if mode == "q":
                            def f_v(e, tt=tt, fj=fj, pslot=pslot, mk_need=s_mk.v):
                                e.wait_ge(s_mk.h, mk_need)
                                e.tensor_scalar(
                                    QT[:, fj * TOWN + tt * 512:fj * TOWN + tt * 512 + 512],
                                    Kp[pslot][:], 0.125, None, ALU.mult).then_inc(s_ek.h, 1)
                            ph.on("vector", f_v)
                            s_ek.v += 1
                        else:
                            def f_v(e, pslot=pslot, mk_need=s_mk.v,
                                    wk_need=s_wk.v):
                                e.wait_ge(s_mk.h, mk_need)
                                if wk_need:
                                    e.wait_ge(s_wk.h, wk_need)
                                e.tensor_copy(kstH[:], Kp[pslot][:])
                                e.tensor_tensor(kstL[:], Kp[pslot][:], kstH[:],
                                                ALU.subtract).then_inc(s_ek.h, 1)
                            ph.on("vector", f_v)
                            s_ek.v += 1

                            def f_s(e, fj=fj, pslot=pslot, ek_need=s_ek.v, t0=t0):
                                e.wait_ge(s_ek.h, ek_need)
                                e.dma_start(out=ktsH[fj * 128:(fj + 1) * 128, t0:t0 + 512],
                                            in_=kstH[:]).then_inc(s_wk.h, 16)
                                e.dma_start(out=ktsL[fj * 128:(fj + 1) * 128, t0:t0 + 512],
                                            in_=kstL[:]).then_inc(s_wk.h, 16)
                            ph.on("sync", f_s)
                            s_wk.v += 32

                if mode == "v":
                    for r in range(4):
                        vv = s_mv.v
                        pslot = vv & 1

                        def f_t(e, r=r, slot=slot, pslot=pslot, xs_need2=xs_need2,
                                ws_need=ws_need, ev_need=max(0, 2 * (s_mv.v - 1))):
                            e.wait_ge(s_xs.h, xs_need2)
                            e.wait_ge(s_ws.h, ws_need)
                            if ev_need:
                                e.wait_ge(s_ev.h, ev_need)
                            for hh in range(2):
                                np_ = 0
                                for ci in range(8):
                                    for xa, wb in ((xTH, WATH[wbuf]), (xTH, WATL[wbuf]),
                                                   (xTL, WATH[wbuf])):
                                        np_ += 1
                                        ins = e.matmul(
                                            Vp[pslot][:, hh * 512:(hh + 1) * 512],
                                            xa[:, slot * 4096 + ci * 512 + r * 128:slot * 4096 + ci * 512 + (r + 1) * 128],
                                            wb[:, ci * 1024 + hh * 512:ci * 1024 + (hh + 1) * 512],
                                            start=(np_ == 1), stop=(np_ == 24))
                                        if np_ == 24 and hh == 1:
                                            ins.then_inc(s_mv.h, 1)
                        ph.on("tensor", f_t)
                        s_mv.v += 1

                        for vh in range(2):
                            def f_v(e, pslot=pslot, vh=vh, mv_need=s_mv.v,
                                    wv_need=s_wv.v):
                                e.wait_ge(s_mv.h, mv_need)
                                if wv_need:
                                    e.wait_ge(s_wv.h, wv_need)
                                e.tensor_copy(vst[:], Vp[pslot][:, vh * 512:(vh + 1) * 512]
                                              ).then_inc(s_ev.h, 1)
                            ph.on("vector", f_v)
                            s_ev.v += 1

                            def f_s(e, t0=t0, r=r, vh=vh, ev_need=s_ev.v):
                                e.wait_ge(s_ev.h, ev_need)
                                e.dma_start(out=vsp[t0 + r * 128:t0 + (r + 1) * 128,
                                                    vh * 512:(vh + 1) * 512],
                                            in_=vst[:]).then_inc(s_wv.h, 16)
                            ph.on("sync", f_s)
                            s_wv.v += 16
                tile_mk[ti] = s_mk.v
                tile_mv[ti] = s_mv.v

            tile_mk, tile_mv = {}, {}
            ti = 0
            buf_last = {0: (0, 0), 1: (0, 0)}
            for w, (wname, mode) in enumerate((("wk", "k"), ("wv", "v"), ("wq", "q"))):
                wbuf = w & 1
                amk, amv = buf_last[wbuf]
                sched_weight_load(wname, wbuf, amk, amv)
                for tt in range(8 if mode in ("k", "v") else 4):
                    sched_tile(ti, tt, mode, wbuf)
                    ti += 1
                buf_last[wbuf] = (s_mk.v, s_mv.v)

            def f_fin2(e):
                e.wait_ge(s_wk.h, s_wk.v)
                e.wait_ge(s_wv.h, s_wv.v)
            ph.on("sync", f_fin2)
            ph.emit()

        if dbg and upto == 2:
            _dump(nc, ctx, [(qtd, QT)])
        if upto == 2:
            return nc

        with ExitStack() as p3:
            kbufH = [p3.enter_context(nc.sbuf_tensor(f"kbufH{i}", [128, L], BF16)) for i in range(2)]
            kbufL = [p3.enter_context(nc.sbuf_tensor(f"kbufL{i}", [128, L], BF16)) for i in range(2)]
            qbufH = [p3.enter_context(nc.sbuf_tensor(f"qbufH{i}", [128, TOWN], BF16)) for i in range(2)]
            qbufL = [p3.enter_context(nc.sbuf_tensor(f"qbufL{i}", [128, TOWN], BF16)) for i in range(2)]
            qtmpF = p3.enter_context(nc.sbuf_tensor("qtmpF", [64, TOWN], F32))
            vbuf = [p3.enter_context(nc.sbuf_tensor(f"vbuf{i}", [128, NSB * 65], F32)) for i in range(2)]
            Abuf = [[p3.enter_context(nc.sbuf_tensor(f"A{i}{j}", [128, 1024], F32))
                     for j in range(2)] for i in range(2)]
            sdp = p3.enter_context(nc.sbuf_tensor("sdp", [128, 1920], F32))
            Sp = [p3.enter_context(nc.psum_tensor(f"Sp{i}", [128, 1024], F32)) for i in range(2)]
            Gp = p3.enter_context(nc.psum_tensor("Gp", [65, 2048], F32))

            s_sts = Sem(ctx, nc, "s_sts")
            s_stg = Sem(ctx, nc, "s_stg")
            s_stc = Sem(ctx, nc, "s_stc")
            s_qsp = Sem(ctx, nc, "s_qsp")
            s_fn = Sem(ctx, nc, "s_fn")
            ek_full = s_ek.v
            s_qk = Sem(ctx, nc, "s_qk")
            s_dg = Sem(ctx, nc, "s_dg")
            s_ex = Sem(ctx, nc, "s_ex")
            s_av = Sem(ctx, nc, "s_av")
            s_gd = Sem(ctx, nc, "s_gd")
            s_dn = Sem(ctx, nc, "s_dn")
            s_hd = Sem(ctx, nc, "s_hd")

            _dgc = {}
            _c = 2 * NSB + 5
            for _h in range(H):
                _c += 1
                for _sb in (list(range(16, NSB)) + list(range(16))):
                    for _kind, _lo, _hi in _subtiles(_sb):
                        if _kind == "diag":
                            _c += 1
                            _dgc[(_h, _sb, _lo // 1024)] = _c
            init_dg = 2 * NSB + 5

            SBORD = list(range(16, NSB)) + list(range(16))
            SBPOS = {sb: i for i, sb in enumerate(SBORD)}

            def ex_idx(h, sb, lh):
                return (h * NSB + SBPOS[sb]) * 2 + lh + 1

            ph = Phase(nc)

            def f_init(e):
                e.memset(sdp[:], 0.0).then_inc(s_dg.h, 1)
                e.wait_ge(s0v.h, 1)
                for i in range(2):
                    e.memset(kbufL[i][64:96, :], 0.0).then_inc(s_dg.h, 1)
                    e.memset(qbufL[i][64:96, :], 0.0).then_inc(s_dg.h, 1)
                    for r, src in ((66, negb), (67, negb), (70, onesb), (71, onesb)):
                        for hf in range(2):
                            e.dma_start(out=kbufH[i][r:r + 1, hf * TOWN:(hf + 1) * TOWN],
                                        in_=src[:]).then_inc(s_stc.h, 16)
                    for r, src in ((64, onesb), (65, onesb), (68, negb), (69, negb)):
                        e.dma_start(out=qbufH[i][r:r + 1, :], in_=src[:]
                                    ).then_inc(s_stc.h, 16)
                for i in range(2):
                    for cc in range(NSB):
                        e.memset(vbuf[i][:, cc * 65 + 64:cc * 65 + 65], 1.0
                                 ).then_inc(s_dg.h, 1)
            ph.on("gpsimd", f_init)
            s_stc.v += 16 * 24
            s_dg.v += 2 * NSB + 5

            stage_sts, stage_stg, stage_qsp = {}, {}, {}

            def sched_stage(h):
                par = h & 1
                hd_need = max(0, h - 1)

                def f_sync(e, h=h, par=par, hd_need=hd_need):
                    if hd_need:
                        e.wait_ge(s_hd.h, hd_need)
                    e.dma_start(out=kbufH[par][0:64, :],
                                in_=ktsH[h * 64:(h + 1) * 64, :]).then_inc(s_sts.h, 16)
                    e.dma_start(out=kbufL[par][0:64, :],
                                in_=ktsL[h * 64:(h + 1) * 64, :]).then_inc(s_sts.h, 16)
                    for cc in range(NSB):
                        e.dma_start(out=vbuf[par][:, cc * 65:cc * 65 + 64],
                                    in_=vsp[cc * 128:(cc + 1) * 128, h * 64:(h + 1) * 64]
                                    ).then_inc(s_sts.h, 16)
                ph.on("sync", f_sync)
                s_sts.v += 16 * (2 + NSB)
                stage_sts[h] = s_sts.v

                def f_gp(e, h=h, par=par, hd_need=hd_need,
                         stage_stg_base=s_stg.v):
                    if hd_need:
                        e.wait_ge(s_hd.h, hd_need)
                    if h < 2:
                        e.wait_ge(s0v.h, 3)
                        e.wait_ge(s0d.h, 32)
                        e.wait_ge(s_ek.h, ek_full)
                    for v2, (rk, rq) in ((0, (64, 66)), (1, (68, 70))):
                        e.dma_start(out=kbufH[par][rk:rk + 1, :],
                                    in_=AUG1[2 * h + v2:2 * h + v2 + 1, :]).then_inc(s_stg.h, 16)
                        e.dma_start(out=kbufH[par][rk + 1:rk + 2, :],
                                    in_=AUG3[2 * h + v2:2 * h + v2 + 1, :]).then_inc(s_stg.h, 16)
                        e.dma_start(out=kbufL[par][rk:rk + 1, :],
                                    in_=AUG2[2 * h + v2:2 * h + v2 + 1, :]).then_inc(s_stg.h, 16)
                        e.dma_start(out=qbufH[par][rq:rq + 1, :],
                                    in_=AUG1[2 * h + v2:2 * h + v2 + 1, 0:TOWN]).then_inc(s_stg.h, 16)
                        e.dma_start(out=qbufH[par][rq + 1:rq + 2, :],
                                    in_=AUG3[2 * h + v2:2 * h + v2 + 1, 0:TOWN]).then_inc(s_stg.h, 16)
                        e.dma_start(out=qbufL[par][rq:rq + 1, :],
                                    in_=AUG2[2 * h + v2:2 * h + v2 + 1, 0:TOWN]).then_inc(s_stg.h, 16)
                    if h % 2:
                        e.dma_start(out=qtmpF[:],
                                    in_=QT[64:128, (h // 2) * TOWN:(h // 2 + 1) * TOWN]
                                    ).then_inc(s_stg.h, 16)
                ph.on("gpsimd", f_gp)
                s_stg.v += 16 * (12 + (h % 2))
                stage_stg[h] = s_stg.v

                def f_qs(e, h=h, par=par, hd_need=hd_need, stg_need=stage_stg[h],
                         ek_need=(ek_full if h < 2 else 0)):
                    if hd_need:
                        e.wait_ge(s_hd.h, hd_need)
                    if ek_need:
                        e.wait_ge(s_ek.h, ek_need)
                    if h % 2:
                        e.wait_ge(s_stg.h, stg_need)
                        qsrc = qtmpF[:]
                    else:
                        qsrc = QT[0:64, (h // 2) * TOWN:(h // 2 + 1) * TOWN]
                    e.tensor_copy(qbufH[par][0:64, :], qsrc)
                    e.tensor_tensor(qbufL[par][0:64, :], qsrc, qbufH[par][0:64, :],
                                    ALU.subtract).then_inc(s_qsp.h, 1)
                ph.on("vector", f_qs)
                s_qsp.v += 1
                stage_qsp[h] = s_qsp.v

            sched_stage(0)
            for h in range(H):
                if h + 1 < H:
                    sched_stage(h + 1)
                par = h & 1

                def f_t(e, h=h, par=par, sts_need=stage_sts[h], stg_need=stage_stg[h],
                        qsp_need=stage_qsp[h]):
                    e.wait_ge(s_sts.h, sts_need)
                    e.wait_ge(s_stg.h, stg_need)
                    e.wait_ge(s_qsp.h, qsp_need)
                    if h == 0:
                        e.wait_ge(s_dg.h, init_dg)
                        e.wait_ge(s_stc.h, 16 * 24)
                    for sb in SBORD:
                        subs = _subtiles(sb)
                        for lh in range(2):
                            prev = ex_idx(h, sb, lh) - 2
                            if prev > 0:
                                e.wait_ge(s_ex.h, prev)
                            last = None
                            for kind, lo, hi in subs:
                                lo2, hi2 = max(lo, lh * 1024), min(hi, (lh + 1) * 1024)
                                if lo2 >= hi2:
                                    continue
                                out_ap = Sp[lh][:, lo2 - lh * 1024:hi2 - lh * 1024]
                                ka = 64 if kind == "diag" else (68 if kind == "low" else 72)
                                ksl = slice(sb * 128, (sb + 1) * 128)
                                e.matmul(out_ap,
                                         kbufH[par][0:ka, ksl],
                                         qbufH[par][0:ka, lo2:hi2],
                                         start=True, stop=False)
                                e.matmul(out_ap,
                                         kbufH[par][0:ka, ksl],
                                         qbufL[par][0:ka, lo2:hi2],
                                         start=False, stop=False, skip_group_check=True)
                                last = e.matmul(out_ap,
                                                kbufL[par][0:ka, ksl],
                                                qbufH[par][0:ka, lo2:hi2],
                                                start=False, stop=True, skip_group_check=True)
                            last.then_inc(s_qk.h, 1)
                        for lh in range(2):
                            if sb == SBORD[0] and h >= 1:
                                e.wait_ge(s_gd.h, h)
                            e.wait_ge(s_ex.h, ex_idx(h, sb, lh))
                            for jj in range(2):
                                ins = e.matmul(
                                    Gp[0:65, lh * 1024 + jj * 512:lh * 1024 + (jj + 1) * 512],
                                    vbuf[par][:, sb * 65:sb * 65 + 65],
                                    Abuf[sb & 1][lh][:, jj * 512:(jj + 1) * 512],
                                    start=(sb == SBORD[0]), stop=(sb == SBORD[-1]))
                                if jj == 1:
                                    ins.then_inc(s_av.h, 1)
                    e.nop().then_inc(s_hd.h, 1)
                ph.on("tensor", f_t)
                s_qk.v += 2 * NSB
                s_av.v += 2 * NSB
                s_hd.v += 1

                def f_v(e, h=h):
                    if h == 0:
                        e.wait_ge(s_dg.h, init_dg)
                    e.tensor_scalar_mul(sdp[:, 896:1024], sdg[:], slopes[h]
                                        ).then_inc(s_dg.h, 1)
                    for sb in SBORD:
                        for kind, lo, hi in _subtiles(sb):
                            if kind != "diag":
                                continue
                            lh = lo // 1024
                            c0 = lo - lh * 1024
                            e.wait_ge(s_qk.h, ex_idx(h, sb, lh))
                            e.tensor_tensor(Sp[lh][:],
                                            Sp[lh][:],
                                            sdp[:, 896 - c0:1920 - c0],
                                            ALU.subtract).then_inc(s_dg.h, 1)
                    e.wait_ge(s_av.h, (h + 1) * NSB * 2)
                    if h >= 1:
                        e.wait_ge(s_dn.h, 16 * h)
                    e.tensor_copy(QT[(h % 2) * 64:(h % 2) * 64 + 64,
                                     (h // 2) * TOWN:(h // 2 + 1) * TOWN],
                                  Gp[0:64, :])
                    e.tensor_copy(dnall[64:65, :], Gp[64:65, :]).then_inc(s_gd.h, 1)
                ph.on("vector", f_v)
                s_dg.v += 1
                for sb in range(NSB):
                    for kind, lo, hi in _subtiles(sb):
                        if kind == "diag":
                            s_dg.v += 1
                s_gd.v += 1

                def f_g2(e, h=h):
                    e.wait_ge(s_gd.h, h + 1)
                    e.dma_start(out=dnall[32 + h:33 + h, :], in_=dnall[64:65, :]
                                ).then_inc(s_dn.h, 16)
                ph.on("gpsimd", f_g2)
                s_dn.v += 16

                def f_sc(e, h=h):
                    npc = 0
                    for sb in SBORD:
                        diag_halves = {lo // 1024 for kind, lo, hi in _subtiles(sb)
                                       if kind == "diag"}
                        for lh in range(2):
                            idx = ex_idx(h, sb, lh)
                            e.wait_ge(s_qk.h, idx)
                            if lh in diag_halves:
                                e.wait_ge(s_dg.h, _dgc[(h, sb, lh)])
                            if idx > 4:
                                e.wait_ge(s_av.h, idx - 4)
                            e.activation(Abuf[sb & 1][lh][:], Sp[lh][:], AF.Exp
                                         ).then_inc(s_ex.h, 1)
                ph.on("scalar", f_sc)
                s_ex.v += NSB * 2
            ph.emit()

        if dbg and upto == 3:
            _dump(nc, ctx, [(qtd, QT)])
        if upto == 3:
            return nc

        with ExitStack() as p4:
            WoT = p4.enter_context(nc.sbuf_tensor("WoT", [128, 8192], F32))
            bob = p4.enter_context(nc.sbuf_tensor("bob", [128, 1024], F32))
            sbo = p4.enter_context(nc.sbuf_tensor("sbo", [1, 1024], F32))
            rtmp = p4.enter_context(nc.sbuf_tensor("rtmp", [1, 2 * TOWN], F32))
            xres = p4.enter_context(nc.sbuf_tensor("xres", [128, 2048], F32))
            ya = p4.enter_context(nc.sbuf_tensor("ya", [128, 1024], F32))
            yb = p4.enter_context(nc.sbuf_tensor("yb", [128, 1024], F32))
            xc4 = p4.enter_context(nc.sbuf_tensor("xc4", [128, 1024], F32))
            x2t = p4.enter_context(nc.sbuf_tensor("x2t", [128, 2048], F32))
            mn4 = p4.enter_context(nc.sbuf_tensor("mn4", [128, 1], F32))
            vr4 = p4.enter_context(nc.sbuf_tensor("vr4", [128, 1], F32))
            sd4 = p4.enter_context(nc.sbuf_tensor("sd4", [128, 1], F32))
            iv4 = p4.enter_context(nc.sbuf_tensor("iv4", [128, 1], F32))

            s4lw = Sem(ctx, nc, "s4lw")
            s4tr = Sem(ctx, nc, "s4tr")
            s4te = Sem(ctx, nc, "s4te")
            s4rg = Sem(ctx, nc, "s4rg")
            s4rv = Sem(ctx, nc, "s4rv")
            s4bc = Sem(ctx, nc, "s4bc")
            s4nm = Sem(ctx, nc, "s4nm")
            s4x = Sem(ctx, nc, "s4x")
            s4m = Sem(ctx, nc, "s4m")
            s4e = Sem(ctx, nc, "s4e")
            s4s = Sem(ctx, nc, "s4s")
            s4w = Sem(ctx, nc, "s4w")

            with nc.psum_tensor("pbc4", [128, 2048], F32) as pbc4, \
                 nc.psum_tensor("pbc4b", [128, 2048], F32) as pbc4b:
                pbc = [pbc4, pbc4b]
                ph = Phase(nc)

                def f_sync(e):
                    e.dma_start(out=sbo[:], in_=bo[:]).then_inc(s4lw.h, 16)
                    for ci in range(8):
                        e.dma_start(out=WoT[:, ci * 1024:(ci + 1) * 1024],
                                    in_=wot[ci * 128:(ci + 1) * 128, :]).then_inc(s4lw.h, 16)
                ph.on("sync", f_sync)
                s4lw.v = 16 * 9

                def f_v0(e):
                    e.wait_ge(s_dn.h, 16 * H)
                    e.reciprocal(dnall[0:16, :], dnall[32:48, :]).then_inc(s4rv.h, 1)
                ph.on("vector", f_v0)

                def f_s0(e):
                    e.wait_ge(s4rv.h, 1)
                    e.activation(mn4[0:16, :], dnall[0:16, 0:1], AF.Identity
                                 ).then_inc(s4rv.h, 1)
                ph.on("scalar", f_s0)
                s4rv.v = 2

                def f_gp(e):
                    e.wait_ge(s4rv.h, 2)
                    for h in range(H):
                        if h >= 2:
                            e.wait_ge(s4bc.h, 2 + h - 1)
                        e.dma_start(out=rtmp[:, (h & 1) * TOWN:((h & 1) + 1) * TOWN],
                                    in_=dnall[h:h + 1, :]).then_inc(s4rg.h, 16)
                ph.on("gpsimd", f_gp)
                s4rg.v = 16 * H

                def f_t(e):
                    e.wait_ge(s4lw.h, 16)
                    for j in range(2):
                        e.matmul(pbc[0][:, j * 512:(j + 1) * 512], ones1[0:1, 0:128],
                                 sbo[0:1, j * 512:(j + 1) * 512], start=True, stop=True
                                 ).then_inc(s4bc.h, 1)
                    for h in range(H):
                        e.wait_ge(s4rg.h, 16 * (h + 1))
                        if h == 0:
                            e.wait_ge(s4nm.h, 1)
                        if h >= 2:
                            e.wait_ge(s4nm.h, h)
                        for j in range(4):
                            ins = e.matmul(pbc[h & 1][:, j * 512:(j + 1) * 512],
                                           ones1[0:1, 0:128],
                                           rtmp[0:1, (h & 1) * TOWN + j * 512:(h & 1) * TOWN + (j + 1) * 512],
                                           start=True, stop=True)
                            if j == 3:
                                ins.then_inc(s4bc.h, 1)
                ph.on("tensor", f_t)
                s4bc.v = 2 + H

                def f_v(e):
                    e.wait_ge(s4bc.h, 2)
                    e.tensor_copy(bob[:], pbc[0][:, 0:1024]).then_inc(s4nm.h, 1)
                    for h in range(H):
                        e.wait_ge(s4bc.h, 2 + h + 1)
                        e.tensor_tensor(QT[(h % 2) * 64:(h % 2) * 64 + 64,
                                           (h // 2) * TOWN:(h // 2 + 1) * TOWN],
                                        QT[(h % 2) * 64:(h % 2) * 64 + 64,
                                           (h // 2) * TOWN:(h // 2 + 1) * TOWN],
                                        pbc[h & 1][(h % 2) * 64:(h % 2) * 64 + 64, :],
                                        ALU.mult).then_inc(s4nm.h, 1)
                ph.on("vector", f_v)
                s4nm.v = 1 + H
                ph.emit()

            with nc.psum_tensor("Wop0", [128, 1024], F32) as Wop0, \
                 nc.psum_tensor("Wop1", [128, 1024], F32) as Wop1:
                Wop = [Wop0, Wop1]
                ph = Phase(nc)

                def f_sync(e):
                    for tb in range(NTB):
                        if tb >= 2:
                            e.wait_ge(s4e.h, tb - 1)
                        e.dma_start(out=xres[:, (tb & 1) * 1024:((tb & 1) + 1) * 1024],
                                    in_=xf[tb * 128:(tb + 1) * 128, :]).then_inc(s4x.h, 16)
                        if tb >= 1:
                            e.wait_ge(s4e.h, tb)
                            e.dma_start(out=x2s[(tb - 1) * 128:tb * 128, :],
                                        in_=x2t[:, ((tb - 1) & 1) * 1024:(((tb - 1) & 1) + 1) * 1024]
                                        ).then_inc(s4w.h, 16)
                    e.wait_ge(s4e.h, NTB)
                    e.dma_start(out=x2s[(NTB - 1) * 128:NTB * 128, :],
                                in_=x2t[:, ((NTB - 1) & 1) * 1024:(((NTB - 1) & 1) + 1) * 1024]
                                ).then_inc(s4w.h, 16)
                    e.wait_ge(s4w.h, 16 * NTB)
                ph.on("sync", f_sync)
                s4x.v = 16 * NTB
                s4w.v = 16 * NTB

                def f_t(e):
                    e.wait_ge(s4lw.h, 16 * 9)
                    for tb in range(NTB):
                        if tb >= 2:
                            e.wait_ge(s4e.h, tb - 1)
                        for hh in range(2):
                            for ci in range(8):
                                ins = e.matmul(
                                    Wop[tb & 1][:, hh * 512:(hh + 1) * 512],
                                    QT[:, ci * TOWN + tb * 128:ci * TOWN + (tb + 1) * 128],
                                    WoT[:, ci * 1024 + hh * 512:ci * 1024 + (hh + 1) * 512],
                                    start=(ci == 0), stop=(ci == 7))
                                if hh == 1 and ci == 7:
                                    ins.then_inc(s4m.h, 1)
                ph.on("tensor", f_t)
                s4m.v = NTB

                def f_v(e):
                    for tb in range(NTB):
                        sl = tb & 1
                        e.wait_ge(s4m.h, tb + 1)
                        e.wait_ge(s4x.h, 16 * (tb + 1))
                        if tb >= 2:
                            e.wait_ge(s4w.h, 16 * (tb - 1))
                        e.tensor_tensor(ya[:], Wop[sl][:], bob[:], ALU.add)
                        e.tensor_tensor(yb[:], ya[:],
                                        xres[:, sl * 1024:(sl + 1) * 1024], ALU.add)
                        e.reduce_sum(mn4[:], yb[:], AXX).then_inc(s4s.h, 1)
                        e.wait_ge(s4s.h, 6 * tb + 2)
                        e.tensor_scalar(xc4[:], yb[:], vr4[:], None, ALU.subtract)
                        e.tensor_tensor(ya[:], xc4[:], xc4[:], ALU.mult)
                        e.reduce_sum(mn4[:], ya[:], AXX).then_inc(s4s.h, 1)
                        e.wait_ge(s4s.h, 6 * tb + 4)
                        e.reciprocal(iv4[:], sd4[:]).then_inc(s4s.h, 1)
                        e.wait_ge(s4s.h, 6 * tb + 6)
                        e.tensor_scalar(x2t[:, sl * 1024:(sl + 1) * 1024], xc4[:],
                                        vr4[:], None, ALU.mult)
                        e.tensor_reduce(amaxc[:, tb:tb + 1],
                                        x2t[:, sl * 1024:(sl + 1) * 1024], AXX,
                                        ALU.max, apply_absolute_value=True
                                        ).then_inc(s4e.h, 1)
                ph.on("vector", f_v)
                s4e.v = NTB

                def f_sc(e):
                    for tb in range(NTB):
                        e.wait_ge(s4s.h, 6 * tb + 1)
                        e.activation(vr4[:], mn4[:], AF.Identity, scale=1.0 / C
                                     ).then_inc(s4s.h, 1)
                        e.wait_ge(s4s.h, 6 * tb + 3)
                        e.activation(sd4[:], mn4[:], AF.Sqrt, bias=sEps[:],
                                     scale=1.0 / C).then_inc(s4s.h, 1)
                        e.wait_ge(s4s.h, 6 * tb + 5)
                        e.activation(vr4[:], iv4[:], AF.Identity).then_inc(s4s.h, 1)
                ph.on("scalar", f_sc)
                s4s.v = 6 * NTB
                ph.emit()

        if dbg and upto == 4:
            _dump(nc, ctx, [(qtd, QT)])
        if upto == 4:
            return nc

        qts.close()

        with ExitStack() as p56:
            xqT = p56.enter_context(nc.sbuf_tensor("xqT", [128, 8 * TOWN], BF16))
            ccst = p56.enter_context(nc.sbuf_tensor("ccst", [128, 128], F32))
            csel = p56.enter_context(nc.sbuf_tensor("csel", [128, 1], F32))
            crow = p56.enter_context(nc.sbuf_tensor("crow", [1, 128], F32))
            g1 = p56.enter_context(nc.sbuf_tensor("g1", [1, 1], F32))
            x2q = p56.enter_context(nc.sbuf_tensor("x2q", [128, 2048], F32))
            qa = p56.enter_context(nc.sbuf_tensor("qa", [128, 1024], F32))
            qb = p56.enter_context(nc.sbuf_tensor("qb", [128, 1024], F32))
            xqt = p56.enter_context(nc.sbuf_tensor("xqt", [128, 2048], BF16))

            s5a = Sem(ctx, nc, "s5a")
            s5f = Sem(ctx, nc, "s5f")
            s5g = Sem(ctx, nc, "s5g")
            s4e_full = s4e.v
            s5l = Sem(ctx, nc, "s5l")
            s5c = Sem(ctx, nc, "s5c")
            s5b = Sem(ctx, nc, "s5b")
            s5t = Sem(ctx, nc, "s5t")
            s5x = Sem(ctx, nc, "s5x")
            s5q = Sem(ctx, nc, "s5q")
            s5z = Sem(ctx, nc, "s5z")

            cselb = p56.enter_context(nc.sbuf_tensor("cselb", [128, 1], F32))
            swfs6 = p56.enter_context(nc.sbuf_tensor("swfs6", [1, 1024], F32))
            sfb6 = p56.enter_context(nc.sbuf_tensor("sfb6", [1, 1024], F32))
            w8sb6 = p56.enter_context(nc.sbuf_tensor("w8sb6", [128, 8192], I8))
            w8bf_ = p56.enter_context(nc.sbuf_tensor("w8bf", [128, 8192], BF16))
            wfT_ = p56.enter_context(nc.sbuf_tensor("wfT", [128, 8192], BF16))
            s6l_ = Sem(ctx, nc, "s6l")
            s6c_ = Sem(ctx, nc, "s6c")
            s6t_ = Sem(ctx, nc, "s6t")
            sG = p56.enter_context(nc.sbuf_tensor("sG", [1, 1], F32))
            sScp = p56.enter_context(nc.sbuf_tensor("sScp", [1, 1], F32))
            psH = nc.psum_tensor("p5b", [128, 2], F32)
            p5b = psH.__enter__()
            if True:
                ph = Phase(nc)

                def f_s_fence(e):
                    e.wait_ge(s4e.h, s4e_full)
                    e.activation(ccst[:, 0:NTB], amaxc[:], AF.Identity
                                 ).then_inc(s5f.h, 1)
                ph.on("scalar", f_s_fence)
                s5f.v = 1

                def f_v_pre(e):
                    e.wait_ge(s5f.h, 1)
                    e.tensor_reduce(csel[:], amaxc[:], AXX, ALU.max).then_inc(s5a.h, 1)
                    e.wait_ge(s5a.h, 2)
                    e.memset(ccst[:], 0.0)
                    e.tensor_copy(ccst[:, 0:1], cselb[:]).then_inc(s5a.h, 1)
                ph.on("vector", f_v_pre)

                def f_s(e):
                    e.wait_ge(s5a.h, 1)
                    e.activation(cselb[:], csel[:], AF.Identity).then_inc(s5a.h, 1)
                    e.wait_ge(s5a.h, 4)
                    e.activation(cselb[:], csel[:], AF.Identity).then_inc(s5a.h, 1)
                    e.wait_ge(s5a.h, 6)
                    e.activation(sG[:], g1[:], AF.Identity, scale=1.0 / 127.0
                                 ).then_inc(s5a.h, 1)
                    e.wait_ge(s5a.h, 8)
                    e.activation(sScp[:], sS[:], AF.Identity).then_inc(s5a.h, 1)
                    e.wait_ge(s5b.h, 1)
                    e.copy(sRb[:], p5b[:, 0:1])
                    e.copy(sSb[:], p5b[:, 1:2]).then_inc(s5a.h, 1)
                ph.on("scalar", f_s)

                def f_sync(e):
                    e.dma_start(out=swfs6[:], in_=wfs[:]).then_inc(s6l_.h, 16)
                    e.dma_start(out=sfb6[:], in_=fbi[:]).then_inc(s6l_.h, 16)
                    for fj in range(8):
                        e.dma_start(out=w8sb6[:, fj * 1024:(fj + 1) * 1024],
                                    in_=wf8[fj * 128:(fj + 1) * 128, :]
                                    ).then_inc(s6l_.h, 16)
                    e.wait_ge(s5a.h, 3)
                    e.dma_start(out=ccin[:], in_=ccst[:]).then_inc(s5l.h, 16)
                ph.on("sync", f_sync)

                def f_gp(e):
                    e.wait_ge(s5l.h, 16)
                    e.collective_compute(
                        "AllReduce", ALU.max,
                        replica_groups=[[0, 1, 2, 3, 4, 5, 6, 7]],
                        ins=[ccin.ap().opt()], outs=[ccout.ap().opt()],
                    ).then_inc(s5c.h, 1)
                    e.wait_ge(s5c.h, 1)
                    e.dma_start(out=ccst[:], in_=ccout[:]).then_inc(s5l.h, 16)
                    e.wait_ge(s5g.h, 1)
                    e.dma_start(out=gtmp[:], in_=ccst[:, 20:21]).then_inc(s5l.h, 16)
                    e.wait_ge(s5l.h, 48)
                    e.dma_start(out=crow[:], in_=gtmp[:]).then_inc(s5l.h, 16)
                ph.on("gpsimd", f_gp)
                s5c.v = 1
                s5l.v = 64

                def f_t(e):
                    e.wait_ge(s5a.h, 10)
                    e.matmul(p5b[:, 0:1], ones1[0:1, 0:128], sRi[0:1, 0:1],
                             start=True, stop=True)
                    e.matmul(p5b[:, 1:2], ones1[0:1, 0:128], sS[0:1, 0:1],
                             start=True, stop=True).then_inc(s5b.h, 1)
                ph.on("tensor", f_t)
                s5b.v = 1
                s5a.v = 11

            def f_conv_early(e):
                e.wait_ge(s6l_.h, 16 * 10)
                for fj in range(8):
                    e.tensor_copy(w8bf_[:, fj * 1024:(fj + 1) * 1024],
                                  w8sb6[:, fj * 1024:(fj + 1) * 1024]).then_inc(s6c_.h, 1)
            ph.on("vector", f_conv_early)

            def f_v_post(e):
                e.wait_ge(s5l.h, 32)
                e.tensor_reduce(csel[:], ccst[:], AXX, ALU.max).then_inc(s5a.h, 1)
                e.wait_ge(s5a.h, 5)
                e.tensor_copy(ccst[:, 20:21], cselb[:]).then_inc(s5g.h, 1)
                e.wait_ge(s5l.h, 64)
                e.tensor_reduce(g1[:], crow[:], AXX, ALU.max).then_inc(s5a.h, 1)
                e.wait_ge(s5a.h, 7)
                e.tensor_scalar(sS[:], sG[:], 1.0, EPS, ALU.mult, ALU.max
                                ).then_inc(s5a.h, 1)
                e.wait_ge(s5a.h, 9)
                e.reciprocal(sRi[:], sScp[:]).then_inc(s5a.h, 1)
            ph.on("vector", f_v_post)

            def f_trans_early(e):
                for fj in range(8):
                    e.wait_ge(s6c_.h, fj + 1)
                    for ci in range(8):
                        e.dma_start_transpose(
                            wfT_[:, ci * 1024 + fj * 128:ci * 1024 + (fj + 1) * 128],
                            w8bf_[:, fj * 1024 + ci * 128:fj * 1024 + (ci + 1) * 128]
                        ).then_inc(s6t_.h, 16)
            ph.on("sync", f_trans_early)
            s6l_.v = 16 * 10
            s6c_.v = 8
            s6t_.v = 16 * 64


            def f_sync(e):
                def trans(tb):
                    e.wait_ge(s5q.h, tb + 1)
                    for ci in range(8):
                        e.dma_start_transpose(
                            xqT[:, ci * TOWN + tb * 128:ci * TOWN + (tb + 1) * 128],
                            xqt[:, (tb & 1) * 1024 + ci * 128:(tb & 1) * 1024 + (ci + 1) * 128]
                        ).then_inc(s5t.h, 16)
                for tb in range(NTB):
                    if tb >= 2:
                        e.wait_ge(s5q.h, tb - 1)
                    e.dma_start(out=x2q[:, (tb & 1) * 1024:((tb & 1) + 1) * 1024],
                                in_=x2s[tb * 128:(tb + 1) * 128, :]).then_inc(s5x.h, 16)
                    if tb >= 1:
                        trans(tb - 1)
                trans(NTB - 1)
            ph.on("sync", f_sync)
            s5x.v = 16 * NTB
            s5t.v = 16 * 8 * NTB

            def f_v(e):
                e.wait_ge(s5a.h, 11)
                for tb in range(NTB):
                    sl = tb & 1
                    e.wait_ge(s5x.h, 16 * (tb + 1))
                    if tb >= 2:
                        e.wait_ge(s5t.h, 16 * 8 * (tb - 1))
                    e.tensor_scalar(qa[:], x2q[:, sl * 1024:(sl + 1) * 1024],
                                    sRb[:], C3, ALU.mult, ALU.add)
                    e.tensor_scalar(qb[:], qa[:], C3, -128.0, ALU.subtract, ALU.max)
                    e.tensor_scalar(xqt[:, sl * 1024:(sl + 1) * 1024], qb[:],
                                    127.0, None, ALU.min).then_inc(s5q.h, 1)
            ph.on("vector", f_v)
            s5q.v = NTB

            if dbg:
                with nc.sbuf_tensor("dbgf", [128, 8 * TOWN], F32) as dbgf:
                    sdx = Sem(ctx, nc, "sdx")
                    ph = Phase(nc)

                    def f_dv(e):
                        for j in range(16):
                            e.tensor_copy(dbgf[:, j * 1024:(j + 1) * 1024],
                                          xqT[:, j * 1024:(j + 1) * 1024])
                        e.tensor_copy(dbgf[0:1, 0:1], sS[:])
                        e.tensor_copy(dbgf[0:1, 1:2], sRi[:]).then_inc(sdx.h, 1)
                    ph.on("vector", f_dv)

                    def f_ds(e):
                        e.wait_ge(sdx.h, 1)
                        e.dma_start(out=qtd[:], in_=dbgf[:]).then_inc(sdx.h, 16)
                        e.wait_ge(sdx.h, 17)
                    ph.on("sync", f_ds)
                    ph.emit()

            with ExitStack() as p6:
                w8bf = w8bf_
                wfT = wfT_
                cb6 = p6.enter_context(nc.sbuf_tensor("cb6", [128, 1024], F32))
                fb6 = p6.enter_context(nc.sbuf_tensor("fb6", [128, 1024], F32))
                x2r = p6.enter_context(nc.sbuf_tensor("x2r", [128, 2048], F32))
                ya6 = p6.enter_context(nc.sbuf_tensor("ya6", [128, 1024], F32))
                yb6 = p6.enter_context(nc.sbuf_tensor("yb6", [128, 1024], F32))
                xc6 = p6.enter_context(nc.sbuf_tensor("xc6", [128, 1024], F32))
                ot6 = p6.enter_context(nc.sbuf_tensor("ot6", [128, 2048], F32))
                mn6 = p6.enter_context(nc.sbuf_tensor("mn6", [128, 1], F32))
                vr6 = p6.enter_context(nc.sbuf_tensor("vr6", [128, 1], F32))
                sd6 = p6.enter_context(nc.sbuf_tensor("sd6", [128, 1], F32))
                iv6 = p6.enter_context(nc.sbuf_tensor("iv6", [128, 1], F32))

                s6l, s6c, s6t = s6l_, s6c_, s6t_
                s6b = Sem(ctx, nc, "s6b")
                s6g = Sem(ctx, nc, "s6g")
                s6x = Sem(ctx, nc, "s6x")
                s6m = Sem(ctx, nc, "s6m")
                s6e = Sem(ctx, nc, "s6e")
                s6s = Sem(ctx, nc, "s6s")
                s6o = Sem(ctx, nc, "s6o")

                with nc.psum_tensor("p6b", [128, 1024], F32) as p6bb, \
                     nc.psum_tensor("Fp0", [128, 1024], F32) as Fp0, \
                     nc.psum_tensor("Fp1", [128, 1024], F32) as Fp1:
                    Fp = [Fp0, Fp1]

                    def f_sync(e):
                        def store(tb):
                            e.wait_ge(s6e.h, tb + 1)
                            e.dma_start(out=out[tb * 128:(tb + 1) * 128, :],
                                        in_=ot6[:, (tb & 1) * 1024:((tb & 1) + 1) * 1024]
                                        ).then_inc(s6o.h, 16)
                        for tb in range(NTB):
                            if tb >= 2:
                                e.wait_ge(s6e.h, tb - 1)
                            e.dma_start(out=x2r[:, (tb & 1) * 1024:((tb & 1) + 1) * 1024],
                                        in_=x2s[tb * 128:(tb + 1) * 128, :]
                                        ).then_inc(s6x.h, 16)
                            if tb >= 1:
                                store(tb - 1)
                        store(NTB - 1)
                        e.wait_ge(s6o.h, 16 * NTB)
                    ph.on("sync", f_sync)
                    s6x.v = 16 * NTB
                    s6o.v = 16 * NTB

                    def f_t(e):
                        e.wait_ge(s6l.h, 32)
                        for j in range(2):
                            e.matmul(p6bb[:, j * 512:(j + 1) * 512], ones1[0:1, 0:128],
                                     swfs6[0:1, j * 512:(j + 1) * 512],
                                     start=True, stop=True).then_inc(s6b.h, 1)
                        e.wait_ge(s6g.h, 1)
                        for j in range(2):
                            e.matmul(p6bb[:, j * 512:(j + 1) * 512], ones1[0:1, 0:128],
                                     sfb6[0:1, j * 512:(j + 1) * 512],
                                     start=True, stop=True).then_inc(s6b.h, 1)
                        e.wait_ge(s6t.h, 16 * 64)
                        e.wait_ge(s5t.h, 16 * 8 * NTB)
                        for tb in range(NTB):
                            if tb >= 2:
                                e.wait_ge(s6e.h, tb - 1)
                            for hh in range(2):
                                for ci in range(8):
                                    ins = e.matmul(
                                        Fp[tb & 1][:, hh * 512:(hh + 1) * 512],
                                        xqT[:, ci * TOWN + tb * 128:ci * TOWN + (tb + 1) * 128],
                                        wfT[:, ci * 1024 + hh * 512:ci * 1024 + (hh + 1) * 512],
                                        start=(ci == 0), stop=(ci == 7))
                                    if hh == 1 and ci == 7:
                                        ins.then_inc(s6m.h, 1)
                    ph.on("tensor", f_t)
                    s6b.v = 4
                    s6m.v = NTB

                    def f_v(e):
                        e.wait_ge(s6b.h, 2)
                        e.tensor_scalar(cb6[:], p6bb[:], sSb[:], None, ALU.mult
                                        ).then_inc(s6g.h, 1)
                        e.wait_ge(s6b.h, 4)
                        e.tensor_copy(fb6[:], p6bb[:])
                        for tb in range(NTB):
                            sl = tb & 1
                            e.wait_ge(s6m.h, tb + 1)
                            e.wait_ge(s6x.h, 16 * (tb + 1))
                            if tb >= 2:
                                e.wait_ge(s6o.h, 16 * (tb - 1))
                            e.tensor_tensor(ya6[:], Fp[sl][:], cb6[:], ALU.mult)
                            e.tensor_tensor(yb6[:], ya6[:], fb6[:], ALU.add)
                            e.tensor_tensor(ya6[:], yb6[:],
                                            x2r[:, sl * 1024:(sl + 1) * 1024], ALU.add)
                            e.reduce_sum(mn6[:], ya6[:], AXX).then_inc(s6s.h, 1)
                            e.wait_ge(s6s.h, 6 * tb + 2)
                            e.tensor_scalar(xc6[:], ya6[:], vr6[:], None, ALU.subtract)
                            e.tensor_tensor(yb6[:], xc6[:], xc6[:], ALU.mult)
                            e.reduce_sum(mn6[:], yb6[:], AXX).then_inc(s6s.h, 1)
                            e.wait_ge(s6s.h, 6 * tb + 4)
                            e.reciprocal(iv6[:], sd6[:]).then_inc(s6s.h, 1)
                            e.wait_ge(s6s.h, 6 * tb + 6)
                            e.tensor_scalar(ot6[:, sl * 1024:(sl + 1) * 1024], xc6[:],
                                            vr6[:], None, ALU.mult).then_inc(s6e.h, 1)
                    ph.on("vector", f_v)
                    s6g.v = 1
                    s6e.v = NTB

                    def f_sc(e):
                        for tb in range(NTB):
                            e.wait_ge(s6s.h, 6 * tb + 1)
                            e.activation(vr6[:], mn6[:], AF.Identity, scale=1.0 / C
                                         ).then_inc(s6s.h, 1)
                            e.wait_ge(s6s.h, 6 * tb + 3)
                            e.activation(sd6[:], mn6[:], AF.Sqrt, bias=sEps[:],
                                         scale=1.0 / C).then_inc(s6s.h, 1)
                            e.wait_ge(s6s.h, 6 * tb + 5)
                            e.activation(vr6[:], iv6[:], AF.Identity).then_inc(s6s.h, 1)
                    ph.on("scalar", f_sc)
                    s6s.v = 6 * NTB
                    ph.emit()
            psH.__exit__(None, None, None)
        return nc
    return nc


def _dump(nc, ctx, pairs):
    sdm = Sem(ctx, nc, f"sdm{len(pairs)}")
    ph = Phase(nc)

    def f(e):
        n = 0
        for dst, src in pairs:
            e.dma_start(out=dst[:], in_=src[:]).then_inc(sdm.h, 16)
            n += 16
        e.wait_ge(sdm.h, n)
    ph.on("sync", f)
    ph.emit()



def _host_constants():
    slopes = _slopes()
    posv = np.arange(L, dtype=np.float32)[None, :]
    i = np.arange(128, dtype=np.float32)
    dgv = np.abs(i[:, None] - i[None, :]).astype(np.float32)
    coefv = np.zeros((32, 1), dtype=np.float32)
    for h in range(H):
        coefv[2 * h + 0, 0] = slopes[h]
        coefv[2 * h + 1, 0] = 2.0 * slopes[h]
    return posv, dgv, coefv


def make_in_maps(x, Wq, Wk, Wv, Wo, bov, W8, wsc, fbv):
    posv, dgv, coefv = _host_constants()
    wqT = np.ascontiguousarray(Wq.T)
    wkT = np.ascontiguousarray(Wk.T)
    wvT = np.ascontiguousarray(Wv.T)
    woT = np.ascontiguousarray(Wo.T)
    in_maps = []
    for c in range(NCORE):
        b, half = divmod(c, 2)
        xb = x[b] if half == 0 else x[b][::-1]
        xb = np.ascontiguousarray(xb)
        in_maps.append({
            "xf": xb, "xft": np.ascontiguousarray(xb.T),
            "wqt": wqT, "wkt": wkT, "wvt": wvT, "wot": woT,
            "bo": bov, "wf8": W8, "wfs": wsc, "fbi": fbv,
            "pos": posv, "dgi": dgv, "coef": coefv,
        })
    return in_maps


def kernel(L0_attn_q, L0_attn_Wq, L0_attn_Wk, L0_attn_Wv, L0_attn_Wo, L0_attn_bo,
           L0_ffn_weight_int8, L0_ffn_weight_scale, L0_ffn_bias):
    x = np.ascontiguousarray(np.asarray(L0_attn_q, dtype=np.float32))
    Wq = np.ascontiguousarray(np.asarray(L0_attn_Wq, dtype=np.float32))
    Wk = np.ascontiguousarray(np.asarray(L0_attn_Wk, dtype=np.float32))
    Wv = np.ascontiguousarray(np.asarray(L0_attn_Wv, dtype=np.float32))
    Wo = np.ascontiguousarray(np.asarray(L0_attn_Wo, dtype=np.float32))
    bov = np.ascontiguousarray(np.asarray(L0_attn_bo, dtype=np.float32).reshape(1, C))
    W8 = np.ascontiguousarray(np.asarray(L0_ffn_weight_int8, dtype=np.int8))
    wsc = np.ascontiguousarray(np.asarray(L0_ffn_weight_scale, dtype=np.float32).reshape(1, C))
    fbv = np.ascontiguousarray(np.asarray(L0_ffn_bias, dtype=np.float32).reshape(1, C))

    in_maps = make_in_maps(x, Wq, Wk, Wv, Wo, bov, W8, wsc, fbv)
    nc = build_nc()
    res = run_bass_kernel_spmd(nc, in_maps, core_ids=list(range(NCORE)))
    outv = np.empty((B, L, C), dtype=np.float32)
    for c in range(NCORE):
        b, half = divmod(c, 2)
        o = np.asarray(res.results[c]["out"])
        if half == 0:
            outv[b, 0:TOWN] = o
        else:
            outv[b, TOWN:L] = o[::-1]
    return outv


# version 40 (final)
# speedup vs baseline: 1.0274x; 1.0173x over previous
import numpy as np
from contextlib import ExitStack

import concourse.bass as bass
import concourse.mybir as mybir
from concourse.bass_utils import run_bass_kernel_spmd

F32 = mybir.dt.float32
BF16 = mybir.dt.bfloat16
I8 = mybir.dt.int8
AF = mybir.ActivationFunctionType
ALU = mybir.AluOpType
AXX = mybir.AxisListType.X

B, L, C, H, D = 4, 4096, 1024, 16, 64
TOWN = 2048
NCORE = 8
C3 = 12582912.0
EPS = 1e-5
NTB = TOWN // 128
NSB = L // 128


def _slopes():
    n = H
    start = 2.0 ** (-(2.0 ** (-(np.log2(n) - 3))))
    return np.array([start * start ** i for i in range(n)], dtype=np.float32)


def _subtiles(sb):
    s0 = sb * 128
    pieces = []
    if s0 >= TOWN:
        pieces.append(("up", 0, TOWN))
    else:
        if s0 > 0:
            pieces.append(("up", 0, s0))
        pieces.append(("diag", s0, s0 + 128))
        if s0 + 128 < TOWN:
            pieces.append(("low", s0 + 128, TOWN))
    res = []
    for kind, lo, hi in pieces:
        a = lo
        while a < hi:
            b2 = min(hi, (a // 512 + 1) * 512)
            res.append((kind, a, b2))
            a = b2
    return res


class Sem:
    def __init__(self, ctx, nc, name):
        self.h = ctx.enter_context(nc.semaphore(name))
        self.v = 0


class Phase:
    ENGS = ("sync", "tensor", "vector", "scalar", "gpsimd")

    def __init__(self, nc):
        self.nc = nc
        self.q = {e: [] for e in self.ENGS}

    def on(self, eng, fn):
        self.q[eng].append(fn)

    def emit(self):
        with self.nc.Block() as blk:
            for e in self.ENGS:
                if self.q[e]:
                    getattr(blk, e)(lambda eng, _f=tuple(self.q[e]): [f(eng) for f in _f])


def build_nc(upto=6, dbg=False):
    slopes = [float(s) for s in _slopes()]
    nc = bass.Bass("TRN2", target_bir_lowering=False)

    xf = nc.dram_tensor("xf", [L, C], F32, kind="ExternalInput")
    xft = nc.dram_tensor("xft", [C, L], F32, kind="ExternalInput")
    wqt = nc.dram_tensor("wqt", [C, C], F32, kind="ExternalInput")
    wkt = nc.dram_tensor("wkt", [C, C], F32, kind="ExternalInput")
    wvt = nc.dram_tensor("wvt", [C, C], F32, kind="ExternalInput")
    wot = nc.dram_tensor("wot", [C, C], F32, kind="ExternalInput")
    bo = nc.dram_tensor("bo", [1, C], F32, kind="ExternalInput")
    wf8 = nc.dram_tensor("wf8", [C, C], I8, kind="ExternalInput")
    wfs = nc.dram_tensor("wfs", [1, C], F32, kind="ExternalInput")
    fbi = nc.dram_tensor("fbi", [1, C], F32, kind="ExternalInput")
    pos = nc.dram_tensor("pos", [1, L], F32, kind="ExternalInput")
    dgi = nc.dram_tensor("dgi", [128, 128], F32, kind="ExternalInput")
    coef = nc.dram_tensor("coef", [32, 1], F32, kind="ExternalInput")
    out = nc.dram_tensor("out", [TOWN, C], F32, kind="ExternalOutput")

    spill = "ExternalOutput" if dbg else "Internal"
    ktsH = nc.dram_tensor("ktsH", [C, L], BF16, kind=spill)
    ktsL = nc.dram_tensor("ktsL", [C, L], BF16, kind=spill)
    vsp = nc.dram_tensor("vsp", [L, C], F32, kind=spill)
    x2s = nc.dram_tensor("x2s", [TOWN, C], F32, kind=spill)
    ccin = nc.dram_tensor("ccin", [128, 128], F32, kind="Internal")
    ccout = nc.dram_tensor("ccout", [128, 128], F32, kind="Internal")
    gtmp = nc.dram_tensor("gtmp", [128], F32, kind="Internal")
    recd = nc.dram_tensor("recd", [16, TOWN], F32, kind="Internal")
    qtd = nc.dram_tensor("qtd", [128, 8 * TOWN], F32, kind="ExternalOutput") if dbg else None
    rcd = nc.dram_tensor("rcd", [16, TOWN], F32, kind="ExternalOutput") if dbg else None

    ctx = ExitStack()
    with ctx:
        ones1 = ctx.enter_context(nc.sbuf_tensor("ones1", [1, 128], F32))
        sS = ctx.enter_context(nc.sbuf_tensor("sS", [1, 1], F32))
        sRi = ctx.enter_context(nc.sbuf_tensor("sRi", [1, 1], F32))
        sRb = ctx.enter_context(nc.sbuf_tensor("sRb", [128, 1], F32))
        sSb = ctx.enter_context(nc.sbuf_tensor("sSb", [128, 1], F32))
        amaxc = ctx.enter_context(nc.sbuf_tensor("amaxc", [128, NTB], F32))
        sEps = ctx.enter_context(nc.sbuf_tensor("sEps", [128, 1], F32))
        qts = ctx.enter_context(ExitStack())
        QT = qts.enter_context(nc.sbuf_tensor("QT", [128, 8 * TOWN], F32))
        sdg = qts.enter_context(nc.sbuf_tensor("sdg", [128, 128], F32))
        dnall = qts.enter_context(nc.sbuf_tensor("dnall", [128, TOWN], F32))
        AUGT = qts.enter_context(nc.sbuf_tensor("AUGT", [96, L], BF16))
        AUG1, AUG2, AUG3 = AUGT[0:32], AUGT[32:64], AUGT[64:96]
        onesb2 = qts.enter_context(nc.sbuf_tensor("onesb2", [33, TOWN], BF16))
        onesb = onesb2[0:1]
        negb = onesb2[32:33]
        p0s = ExitStack()
        RAMP = p0s.enter_context(nc.sbuf_tensor("RAMP", [32, L], F32))
        spos = p0s.enter_context(nc.sbuf_tensor("spos", [1, L], F32))
        rs1 = p0s.enter_context(nc.sbuf_tensor("rs1", [32, L], F32))
        rs2 = p0s.enter_context(nc.sbuf_tensor("rs2", [32, L], F32))
        scoef = p0s.enter_context(nc.sbuf_tensor("scoef", [32, 1], F32))
        augt2 = p0s.enter_context(nc.sbuf_tensor("augt2", [32, L], BF16))
        augt3 = p0s.enter_context(nc.sbuf_tensor("augt3", [32, L], BF16))

        s0ld = Sem(ctx, nc, "s0ld")
        s0t = Sem(ctx, nc, "s0t")
        s0v = Sem(ctx, nc, "s0v")
        s0d = Sem(ctx, nc, "s0d")
        ph = Phase(nc)

        def _p0_sync(e):
            for dst, src in ((spos, pos), (sdg, dgi), (scoef, coef)):
                e.dma_start(out=dst[:], in_=src[:]).then_inc(s0ld.h, 16)
        ph.on("sync", _p0_sync)
        s0ld.v = 48

        def _p0_vec(e):
            e.memset(ones1[:], 1.0)
            e.memset(sEps[:], EPS)
        ph.on("vector", _p0_vec)
        ph.emit()

        with nc.psum_tensor("pbc0", [32, 2048], F32) as pbc0:
            ph = Phase(nc)

            def _p0b_t(e):
                e.wait_ge(s0ld.h, 48)
                for half in range(2):
                    if half == 1:
                        e.wait_ge(s0v.h, 1)
                    for j in range(4):
                        e.matmul(pbc0[:, j * 512:(j + 1) * 512], ones1[0:1, 0:32],
                                 spos[0:1, half * 2048 + j * 512:half * 2048 + (j + 1) * 512],
                                 start=True, stop=True).then_inc(s0t.h, 1)
            ph.on("tensor", _p0b_t)
            s0t.v = 8

            def _p0b_v(e):
                e.memset(onesb2[0:1, :], 1.0)
                e.memset(onesb2[32:33, :], -1.0)
                for half in range(2):
                    e.wait_ge(s0t.h, 4 * (half + 1))
                    e.tensor_scalar(RAMP[:, half * 2048:(half + 1) * 2048], pbc0[:],
                                    scoef[:], None, ALU.mult).then_inc(s0v.h, 1)
                e.tensor_copy(AUG1[:], RAMP[:])
                e.tensor_tensor(rs1[:], RAMP[:], AUG1[:], ALU.subtract)
                e.tensor_copy(augt2[:], rs1[:])
                e.tensor_tensor(rs2[:], rs1[:], augt2[:], ALU.subtract)
                e.tensor_copy(augt3[:], rs2[:]).then_inc(s0v.h, 1)
            ph.on("vector", _p0b_v)
            s0v.v = 3

            def _p0b_s(e):
                e.wait_ge(s0v.h, 3)
                e.dma_start(out=AUGT[32:64, :], in_=augt2[:]).then_inc(s0d.h, 16)
                e.dma_start(out=AUGT[64:96, :], in_=augt3[:]).then_inc(s0d.h, 16)
            ph.on("scalar", _p0b_s)
            s0d.v = 32
            ph.emit()
        p0s.close()

        with ExitStack() as p2:
            WATH = [p2.enter_context(nc.sbuf_tensor(f"WATH{i}", [128, 8192], BF16))
                    for i in range(2)]
            WATL = [p2.enter_context(nc.sbuf_tensor(f"WATL{i}", [128, 8192], BF16))
                    for i in range(2)]
            wldf = p2.enter_context(nc.sbuf_tensor("wldf", [128, 1024], F32))
            xTF = p2.enter_context(nc.sbuf_tensor("xTF", [128, 4096], F32))
            xTH = p2.enter_context(nc.sbuf_tensor("xTH", [128, 8192], BF16))
            xTL = p2.enter_context(nc.sbuf_tensor("xTL", [128, 8192], BF16))
            kstH = p2.enter_context(nc.sbuf_tensor("kstH", [128, 512], BF16))
            vst = p2.enter_context(nc.sbuf_tensor("vst", [128, 512], F32))
            kstL = p2.enter_context(nc.sbuf_tensor("kstL", [128, 512], BF16))
            Kp = [p2.enter_context(nc.psum_tensor(f"Kp{i}", [128, 512], F32)) for i in range(2)]
            Vp = [p2.enter_context(nc.psum_tensor(f"Vp{i}", [128, 1024], F32)) for i in range(2)]

            s_lw = Sem(ctx, nc, "s_lw")
            s_lx = Sem(ctx, nc, "s_lx")
            s_ws = Sem(ctx, nc, "s_ws")
            s_xs = Sem(ctx, nc, "s_xs")
            s_mk = Sem(ctx, nc, "s_mk")
            s_ek = Sem(ctx, nc, "s_ek")
            s_wk = Sem(ctx, nc, "s_wk")
            s_mv = Sem(ctx, nc, "s_mv")
            s_ev = Sem(ctx, nc, "s_ev")
            s_wv = Sem(ctx, nc, "s_wv")

            ph = Phase(nc)
            wsrc_map = {"wk": wkt, "wv": wvt, "wq": wqt}

            def sched_weight_load(wname, wbuf, after_mk, after_mv):
                wsrc = wsrc_map[wname]
                ws_base = s_ws.v

                def f_sync(e, wsrc=wsrc, after_mk=after_mk, after_mv=after_mv,
                           ws_base=ws_base, first=(s_lw.v == 0)):
                    if first:
                        e.wait_ge(s0d.h, 32)
                    for ci in range(8):
                        if ws_base + ci > 0:
                            e.wait_ge(s_ws.h, ws_base + ci)
                        e.dma_start(out=wldf[:],
                                    in_=wsrc[ci * 128:(ci + 1) * 128, :]).then_inc(s_lw.h, 16)
                ph.on("gpsimd", f_sync)
                lw_base = s_lw.v
                s_lw.v += 16 * 8

                splits = []
                for ci in range(8):
                    def f_vsp(e, ci=ci, lw_base=lw_base, wbuf=wbuf,
                              after_mk=after_mk, after_mv=after_mv):
                        if ci == 0:
                            if after_mk:
                                e.wait_ge(s_mk.h, after_mk)
                            if after_mv:
                                e.wait_ge(s_mv.h, after_mv)
                        e.wait_ge(s_lw.h, lw_base + 16 * (ci + 1))
                        e.tensor_copy(WATH[wbuf][:, ci * 1024:(ci + 1) * 1024],
                                      wldf[:])
                        e.tensor_tensor(WATL[wbuf][:, ci * 1024:(ci + 1) * 1024],
                                        wldf[:],
                                        WATH[wbuf][:, ci * 1024:(ci + 1) * 1024],
                                        ALU.subtract).then_inc(s_ws.h, 1)
                    splits.append(f_vsp)
                s_ws.v += 8
                return splits

            def sched_tile(ti, tt, mode, wbuf, ws_need_w):
                slot = ti & 1
                t0 = tt * 512
                mk_prev2 = max(0, s_mk.v - (8 if mode in ("k", "q") else 0) * 1)
                need_mk = tile_mk.get(ti - 2, 0)
                need_mv = tile_mv.get(ti - 2, 0)

                xs0 = s_xs.v
                for hf in range(2):
                    fslot = (2 * ti + hf) & 1
                    def f_sync(e, t0=t0, hf=hf, fslot=fslot,
                               xs_need=max(0, s_xs.v - 1)):
                        if xs_need:
                            e.wait_ge(s_xs.h, xs_need)
                        for ci4 in range(4):
                            ci = hf * 4 + ci4
                            e.dma_start(out=xTF[:, fslot * 2048 + ci4 * 512:
                                                fslot * 2048 + (ci4 + 1) * 512],
                                        in_=xft[ci * 128:(ci + 1) * 128, t0:t0 + 512]
                                        ).then_inc(s_lx.h, 16)
                    ph.on("scalar", f_sync)
                    s_lx.v += 16 * 4
                    lx_need = s_lx.v

                    def f_xsp(e, slot=slot, hf=hf, fslot=fslot, lx_need=lx_need,
                              need_mk=need_mk, need_mv=need_mv):
                        e.wait_ge(s_lx.h, lx_need)
                        if hf == 0:
                            if need_mk:
                                e.wait_ge(s_mk.h, need_mk)
                            if need_mv:
                                e.wait_ge(s_mv.h, need_mv)
                        o = slot * 4096 + hf * 2048
                        fo = fslot * 2048
                        e.tensor_copy(xTH[:, o:o + 2048], xTF[:, fo:fo + 2048])
                        e.tensor_tensor(xTL[:, o:o + 2048], xTF[:, fo:fo + 2048],
                                        xTH[:, o:o + 2048],
                                        ALU.subtract).then_inc(s_xs.h, 1)
                    ph.on("vector", f_xsp)
                    s_xs.v += 1
                xs_need2 = s_xs.v
                ws_need = ws_need_w

                if mode in ("k", "q"):
                    for fj in range(8):
                        kk = s_mk.v
                        pslot = kk & 1

                        def f_t(e, fj=fj, slot=slot, pslot=pslot, xs_need2=xs_need2,
                                ws_need=ws_need, ek_need=max(0, s_ek.v - 1)):
                            e.wait_ge(s_xs.h, xs_need2)
                            e.wait_ge(s_ws.h, ws_need)
                            if ek_need:
                                e.wait_ge(s_ek.h, ek_need)
                            np_ = 0
                            for ci in range(8):
                                for wa, xb in ((WATH[wbuf], xTH), (WATH[wbuf], xTL),
                                               (WATL[wbuf], xTH)):
                                    np_ += 1
                                    ins = e.matmul(
                                        Kp[pslot][:],
                                        wa[:, ci * 1024 + fj * 128:ci * 1024 + (fj + 1) * 128],
                                        xb[:, slot * 4096 + ci * 512:slot * 4096 + (ci + 1) * 512],
                                        start=(np_ == 1), stop=(np_ == 24))
                                    if np_ == 24:
                                        ins.then_inc(s_mk.h, 1)
                        ph.on("tensor", f_t)
                        s_mk.v += 1

                        if mode == "q":
                            def f_v(e, tt=tt, fj=fj, pslot=pslot, mk_need=s_mk.v):
                                e.wait_ge(s_mk.h, mk_need)
                                e.tensor_scalar(
                                    QT[:, fj * TOWN + tt * 512:fj * TOWN + tt * 512 + 512],
                                    Kp[pslot][:], 0.125, None, ALU.mult).then_inc(s_ek.h, 1)
                            ph.on("vector", f_v)
                            s_ek.v += 1
                        else:
                            def f_v(e, pslot=pslot, mk_need=s_mk.v,
                                    wk_need=s_wk.v):
                                e.wait_ge(s_mk.h, mk_need)
                                if wk_need:
                                    e.wait_ge(s_wk.h, wk_need)
                                e.tensor_copy(kstH[:], Kp[pslot][:])
                                e.tensor_tensor(kstL[:], Kp[pslot][:], kstH[:],
                                                ALU.subtract).then_inc(s_ek.h, 1)
                            ph.on("vector", f_v)
                            s_ek.v += 1

                            def f_s(e, fj=fj, pslot=pslot, ek_need=s_ek.v, t0=t0):
                                e.wait_ge(s_ek.h, ek_need)
                                e.dma_start(out=ktsH[fj * 128:(fj + 1) * 128, t0:t0 + 512],
                                            in_=kstH[:]).then_inc(s_wk.h, 16)
                                e.dma_start(out=ktsL[fj * 128:(fj + 1) * 128, t0:t0 + 512],
                                            in_=kstL[:]).then_inc(s_wk.h, 16)
                            ph.on("sync", f_s)
                            s_wk.v += 32

                if mode == "v":
                    for r in range(4):
                        vv = s_mv.v
                        pslot = vv & 1

                        def f_t(e, r=r, slot=slot, pslot=pslot, xs_need2=xs_need2,
                                ws_need=ws_need, ev_need=max(0, 2 * (s_mv.v - 1))):
                            e.wait_ge(s_xs.h, xs_need2)
                            e.wait_ge(s_ws.h, ws_need)
                            if ev_need:
                                e.wait_ge(s_ev.h, ev_need)
                            for hh in range(2):
                                np_ = 0
                                for ci in range(8):
                                    for xa, wb in ((xTH, WATH[wbuf]), (xTH, WATL[wbuf]),
                                                   (xTL, WATH[wbuf])):
                                        np_ += 1
                                        ins = e.matmul(
                                            Vp[pslot][:, hh * 512:(hh + 1) * 512],
                                            xa[:, slot * 4096 + ci * 512 + r * 128:slot * 4096 + ci * 512 + (r + 1) * 128],
                                            wb[:, ci * 1024 + hh * 512:ci * 1024 + (hh + 1) * 512],
                                            start=(np_ == 1), stop=(np_ == 24))
                                        if np_ == 24 and hh == 1:
                                            ins.then_inc(s_mv.h, 1)
                        ph.on("tensor", f_t)
                        s_mv.v += 1

                        for vh in range(2):
                            def f_v(e, pslot=pslot, vh=vh, mv_need=s_mv.v,
                                    wv_need=s_wv.v):
                                e.wait_ge(s_mv.h, mv_need)
                                if wv_need:
                                    e.wait_ge(s_wv.h, wv_need)
                                e.tensor_copy(vst[:], Vp[pslot][:, vh * 512:(vh + 1) * 512]
                                              ).then_inc(s_ev.h, 1)
                            ph.on("vector", f_v)
                            s_ev.v += 1

                            def f_s(e, t0=t0, r=r, vh=vh, ev_need=s_ev.v):
                                e.wait_ge(s_ev.h, ev_need)
                                e.dma_start(out=vsp[t0 + r * 128:t0 + (r + 1) * 128,
                                                    vh * 512:(vh + 1) * 512],
                                            in_=vst[:]).then_inc(s_wv.h, 16)
                            ph.on("sync", f_s)
                            s_wv.v += 16
                tile_mk[ti] = s_mk.v
                tile_mv[ti] = s_mv.v

            tile_mk, tile_mv = {}, {}
            ti = 0
            buf_last = {0: (0, 0), 1: (0, 0)}
            wdefs = (("wk", "k", 8), ("wv", "v", 8), ("wq", "q", 4))
            sp0 = sched_weight_load("wk", 0, 0, 0)
            for sp in sp0:
                ph.on("vector", sp)
            ws_by_w = {0: s_ws.v}
            for w, (wname, mode, nt) in enumerate(wdefs):
                wbuf = w & 1
                if w + 1 < len(wdefs):
                    nb = (w + 1) & 1
                    amk, amv = buf_last[nb]
                    nxt = sched_weight_load(wdefs[w + 1][0], nb, amk, amv)
                    ws_by_w[w + 1] = s_ws.v
                else:
                    nxt = []
                for tt in range(nt):
                    sched_tile(ti, tt, mode, wbuf, ws_by_w[w])
                    if nxt:
                        ph.on("vector", nxt.pop(0))
                    ti += 1
                for sp in nxt:
                    ph.on("vector", sp)
                buf_last[wbuf] = (s_mk.v, s_mv.v)

            def f_fin2(e):
                e.wait_ge(s_wk.h, s_wk.v)
                e.wait_ge(s_wv.h, s_wv.v)
            ph.on("sync", f_fin2)
            ph.emit()

        if dbg and upto == 2:
            _dump(nc, ctx, [(qtd, QT)])
        if upto == 2:
            return nc

        with ExitStack() as p3:
            kbufH = [p3.enter_context(nc.sbuf_tensor(f"kbufH{i}", [128, L], BF16)) for i in range(2)]
            kbufL = [p3.enter_context(nc.sbuf_tensor(f"kbufL{i}", [128, L], BF16)) for i in range(2)]
            qbufH = [p3.enter_context(nc.sbuf_tensor(f"qbufH{i}", [128, TOWN], BF16)) for i in range(2)]
            qbufL = [p3.enter_context(nc.sbuf_tensor(f"qbufL{i}", [128, TOWN], BF16)) for i in range(2)]
            qtmpF = p3.enter_context(nc.sbuf_tensor("qtmpF", [64, TOWN], F32))
            vbuf = [p3.enter_context(nc.sbuf_tensor(f"vbuf{i}", [128, NSB * 65], F32)) for i in range(2)]
            Abuf = [[p3.enter_context(nc.sbuf_tensor(f"A{i}{j}", [128, 1024], F32))
                     for j in range(2)] for i in range(2)]
            sdp = p3.enter_context(nc.sbuf_tensor("sdp", [128, 1920], F32))
            Sp = [p3.enter_context(nc.psum_tensor(f"Sp{i}", [128, 1024], F32)) for i in range(2)]
            Gp = p3.enter_context(nc.psum_tensor("Gp", [65, 2048], F32))

            s_sts = Sem(ctx, nc, "s_sts")
            s_stg = Sem(ctx, nc, "s_stg")
            s_stc = Sem(ctx, nc, "s_stc")
            s_qsp = Sem(ctx, nc, "s_qsp")
            s_fn = Sem(ctx, nc, "s_fn")
            ek_full = s_ek.v
            s_qk = Sem(ctx, nc, "s_qk")
            s_dg = Sem(ctx, nc, "s_dg")
            s_ex = Sem(ctx, nc, "s_ex")
            s_av = Sem(ctx, nc, "s_av")
            s_gd = Sem(ctx, nc, "s_gd")
            s_dn = Sem(ctx, nc, "s_dn")
            s_hd = Sem(ctx, nc, "s_hd")

            _dgc = {}
            _c = 2 * NSB + 5
            for _h in range(H):
                _c += 1
                for _sb in (list(range(16, NSB)) + list(range(16))):
                    for _kind, _lo, _hi in _subtiles(_sb):
                        if _kind == "diag":
                            _c += 1
                            _dgc[(_h, _sb, _lo // 1024)] = _c
            init_dg = 2 * NSB + 5

            SBORD = list(range(16, NSB)) + list(range(16))
            SBPOS = {sb: i for i, sb in enumerate(SBORD)}

            def ex_idx(h, sb, lh):
                return (h * NSB + SBPOS[sb]) * 2 + lh + 1

            ph = Phase(nc)

            def f_init(e):
                e.memset(sdp[:], 0.0).then_inc(s_dg.h, 1)
                e.wait_ge(s0v.h, 1)
                for i in range(2):
                    e.memset(kbufL[i][64:96, :], 0.0).then_inc(s_dg.h, 1)
                    e.memset(qbufL[i][64:96, :], 0.0).then_inc(s_dg.h, 1)
                    for r, src in ((66, negb), (67, negb), (70, onesb), (71, onesb)):
                        for hf in range(2):
                            e.dma_start(out=kbufH[i][r:r + 1, hf * TOWN:(hf + 1) * TOWN],
                                        in_=src[:]).then_inc(s_stc.h, 16)
                    for r, src in ((64, onesb), (65, onesb), (68, negb), (69, negb)):
                        e.dma_start(out=qbufH[i][r:r + 1, :], in_=src[:]
                                    ).then_inc(s_stc.h, 16)
                for i in range(2):
                    for cc in range(NSB):
                        e.memset(vbuf[i][:, cc * 65 + 64:cc * 65 + 65], 1.0
                                 ).then_inc(s_dg.h, 1)
            ph.on("gpsimd", f_init)
            s_stc.v += 16 * 24
            s_dg.v += 2 * NSB + 5

            stage_sts, stage_stg, stage_qsp = {}, {}, {}

            def sched_stage(h):
                par = h & 1
                hd_need = max(0, h - 1)

                def f_sync(e, h=h, par=par, hd_need=hd_need):
                    if hd_need:
                        e.wait_ge(s_hd.h, hd_need)
                    e.dma_start(out=kbufH[par][0:64, :],
                                in_=ktsH[h * 64:(h + 1) * 64, :]).then_inc(s_sts.h, 16)
                    e.dma_start(out=kbufL[par][0:64, :],
                                in_=ktsL[h * 64:(h + 1) * 64, :]).then_inc(s_sts.h, 16)
                    for cc in range(NSB):
                        e.dma_start(out=vbuf[par][:, cc * 65:cc * 65 + 64],
                                    in_=vsp[cc * 128:(cc + 1) * 128, h * 64:(h + 1) * 64]
                                    ).then_inc(s_sts.h, 16)
                ph.on("sync", f_sync)
                s_sts.v += 16 * (2 + NSB)
                stage_sts[h] = s_sts.v

                def f_gp(e, h=h, par=par, hd_need=hd_need,
                         stage_stg_base=s_stg.v):
                    if hd_need:
                        e.wait_ge(s_hd.h, hd_need)
                    if h < 2:
                        e.wait_ge(s0v.h, 3)
                        e.wait_ge(s0d.h, 32)
                        e.wait_ge(s_ek.h, ek_full)
                    for v2, (rk, rq) in ((0, (64, 66)), (1, (68, 70))):
                        e.dma_start(out=kbufH[par][rk:rk + 1, :],
                                    in_=AUG1[2 * h + v2:2 * h + v2 + 1, :]).then_inc(s_stg.h, 16)
                        e.dma_start(out=kbufH[par][rk + 1:rk + 2, :],
                                    in_=AUG3[2 * h + v2:2 * h + v2 + 1, :]).then_inc(s_stg.h, 16)
                        e.dma_start(out=kbufL[par][rk:rk + 1, :],
                                    in_=AUG2[2 * h + v2:2 * h + v2 + 1, :]).then_inc(s_stg.h, 16)
                        e.dma_start(out=qbufH[par][rq:rq + 1, :],
                                    in_=AUG1[2 * h + v2:2 * h + v2 + 1, 0:TOWN]).then_inc(s_stg.h, 16)
                        e.dma_start(out=qbufH[par][rq + 1:rq + 2, :],
                                    in_=AUG3[2 * h + v2:2 * h + v2 + 1, 0:TOWN]).then_inc(s_stg.h, 16)
                        e.dma_start(out=qbufL[par][rq:rq + 1, :],
                                    in_=AUG2[2 * h + v2:2 * h + v2 + 1, 0:TOWN]).then_inc(s_stg.h, 16)
                    if h % 2:
                        e.dma_start(out=qtmpF[:],
                                    in_=QT[64:128, (h // 2) * TOWN:(h // 2 + 1) * TOWN]
                                    ).then_inc(s_stg.h, 16)
                ph.on("gpsimd", f_gp)
                s_stg.v += 16 * (12 + (h % 2))
                stage_stg[h] = s_stg.v

                def f_qs(e, h=h, par=par, hd_need=hd_need, stg_need=stage_stg[h],
                         ek_need=(ek_full if h < 2 else 0)):
                    if hd_need:
                        e.wait_ge(s_hd.h, hd_need)
                    if ek_need:
                        e.wait_ge(s_ek.h, ek_need)
                    if h % 2:
                        e.wait_ge(s_stg.h, stg_need)
                        qsrc = qtmpF[:]
                    else:
                        qsrc = QT[0:64, (h // 2) * TOWN:(h // 2 + 1) * TOWN]
                    e.tensor_copy(qbufH[par][0:64, :], qsrc)
                    e.tensor_tensor(qbufL[par][0:64, :], qsrc, qbufH[par][0:64, :],
                                    ALU.subtract).then_inc(s_qsp.h, 1)
                ph.on("vector", f_qs)
                s_qsp.v += 1
                stage_qsp[h] = s_qsp.v

            sched_stage(0)
            for h in range(H):
                if h + 1 < H:
                    sched_stage(h + 1)
                par = h & 1

                def f_t(e, h=h, par=par, sts_need=stage_sts[h], stg_need=stage_stg[h],
                        qsp_need=stage_qsp[h]):
                    e.wait_ge(s_sts.h, sts_need)
                    e.wait_ge(s_stg.h, stg_need)
                    e.wait_ge(s_qsp.h, qsp_need)
                    if h == 0:
                        e.wait_ge(s_dg.h, init_dg)
                        e.wait_ge(s_stc.h, 16 * 24)
                    for sb in SBORD:
                        subs = _subtiles(sb)
                        for lh in range(2):
                            prev = ex_idx(h, sb, lh) - 2
                            if prev > 0:
                                e.wait_ge(s_ex.h, prev)
                            last = None
                            for kind, lo, hi in subs:
                                lo2, hi2 = max(lo, lh * 1024), min(hi, (lh + 1) * 1024)
                                if lo2 >= hi2:
                                    continue
                                out_ap = Sp[lh][:, lo2 - lh * 1024:hi2 - lh * 1024]
                                ka = 64 if kind == "diag" else (68 if kind == "low" else 72)
                                ksl = slice(sb * 128, (sb + 1) * 128)
                                e.matmul(out_ap,
                                         kbufH[par][0:ka, ksl],
                                         qbufH[par][0:ka, lo2:hi2],
                                         start=True, stop=False)
                                e.matmul(out_ap,
                                         kbufH[par][0:ka, ksl],
                                         qbufL[par][0:ka, lo2:hi2],
                                         start=False, stop=False, skip_group_check=True)
                                last = e.matmul(out_ap,
                                                kbufL[par][0:ka, ksl],
                                                qbufH[par][0:ka, lo2:hi2],
                                                start=False, stop=True, skip_group_check=True)
                            last.then_inc(s_qk.h, 1)
                        for lh in range(2):
                            if sb == SBORD[0] and h >= 1:
                                e.wait_ge(s_gd.h, h)
                            e.wait_ge(s_ex.h, ex_idx(h, sb, lh))
                            for jj in range(2):
                                ins = e.matmul(
                                    Gp[0:65, lh * 1024 + jj * 512:lh * 1024 + (jj + 1) * 512],
                                    vbuf[par][:, sb * 65:sb * 65 + 65],
                                    Abuf[sb & 1][lh][:, jj * 512:(jj + 1) * 512],
                                    start=(sb == SBORD[0]), stop=(sb == SBORD[-1]))
                                if jj == 1:
                                    ins.then_inc(s_av.h, 1)
                    e.nop().then_inc(s_hd.h, 1)
                ph.on("tensor", f_t)
                s_qk.v += 2 * NSB
                s_av.v += 2 * NSB
                s_hd.v += 1

                def f_v(e, h=h):
                    if h == 0:
                        e.wait_ge(s_dg.h, init_dg)
                    e.tensor_scalar_mul(sdp[:, 896:1024], sdg[:], slopes[h]
                                        ).then_inc(s_dg.h, 1)
                    for sb in SBORD:
                        for kind, lo, hi in _subtiles(sb):
                            if kind != "diag":
                                continue
                            lh = lo // 1024
                            c0 = lo - lh * 1024
                            e.wait_ge(s_qk.h, ex_idx(h, sb, lh))
                            e.tensor_tensor(Sp[lh][:],
                                            Sp[lh][:],
                                            sdp[:, 896 - c0:1920 - c0],
                                            ALU.subtract).then_inc(s_dg.h, 1)
                    e.wait_ge(s_av.h, (h + 1) * NSB * 2)
                    if h >= 1:
                        e.wait_ge(s_dn.h, 16 * h)
                    e.tensor_copy(QT[(h % 2) * 64:(h % 2) * 64 + 64,
                                     (h // 2) * TOWN:(h // 2 + 1) * TOWN],
                                  Gp[0:64, :])
                    e.tensor_copy(dnall[64:65, :], Gp[64:65, :]).then_inc(s_gd.h, 1)
                ph.on("vector", f_v)
                s_dg.v += 1
                for sb in range(NSB):
                    for kind, lo, hi in _subtiles(sb):
                        if kind == "diag":
                            s_dg.v += 1
                s_gd.v += 1

                def f_g2(e, h=h):
                    e.wait_ge(s_gd.h, h + 1)
                    e.dma_start(out=dnall[32 + h:33 + h, :], in_=dnall[64:65, :]
                                ).then_inc(s_dn.h, 16)
                ph.on("gpsimd", f_g2)
                s_dn.v += 16

                def f_sc(e, h=h):
                    npc = 0
                    for sb in SBORD:
                        diag_halves = {lo // 1024 for kind, lo, hi in _subtiles(sb)
                                       if kind == "diag"}
                        for lh in range(2):
                            idx = ex_idx(h, sb, lh)
                            e.wait_ge(s_qk.h, idx)
                            if lh in diag_halves:
                                e.wait_ge(s_dg.h, _dgc[(h, sb, lh)])
                            if idx > 4:
                                e.wait_ge(s_av.h, idx - 4)
                            e.activation(Abuf[sb & 1][lh][:], Sp[lh][:], AF.Exp
                                         ).then_inc(s_ex.h, 1)
                ph.on("scalar", f_sc)
                s_ex.v += NSB * 2
            ph.emit()

        if dbg and upto == 3:
            _dump(nc, ctx, [(qtd, QT)])
        if upto == 3:
            return nc

        with ExitStack() as p4:
            WoT = p4.enter_context(nc.sbuf_tensor("WoT", [128, 8192], F32))
            bob = p4.enter_context(nc.sbuf_tensor("bob", [128, 1024], F32))
            sbo = p4.enter_context(nc.sbuf_tensor("sbo", [1, 1024], F32))
            rtmp = p4.enter_context(nc.sbuf_tensor("rtmp", [1, 2 * TOWN], F32))
            xres = p4.enter_context(nc.sbuf_tensor("xres", [128, 2048], F32))
            ya = p4.enter_context(nc.sbuf_tensor("ya", [128, 1024], F32))
            yb = p4.enter_context(nc.sbuf_tensor("yb", [128, 1024], F32))
            xc4 = p4.enter_context(nc.sbuf_tensor("xc4", [128, 1024], F32))
            x2t = p4.enter_context(nc.sbuf_tensor("x2t", [128, 2048], F32))
            mn4 = p4.enter_context(nc.sbuf_tensor("mn4", [128, 1], F32))
            vr4 = p4.enter_context(nc.sbuf_tensor("vr4", [128, 1], F32))
            sd4 = p4.enter_context(nc.sbuf_tensor("sd4", [128, 1], F32))
            iv4 = p4.enter_context(nc.sbuf_tensor("iv4", [128, 1], F32))

            s4lw = Sem(ctx, nc, "s4lw")
            s4tr = Sem(ctx, nc, "s4tr")
            s4te = Sem(ctx, nc, "s4te")
            s4rg = Sem(ctx, nc, "s4rg")
            s4rv = Sem(ctx, nc, "s4rv")
            s4bc = Sem(ctx, nc, "s4bc")
            s4nm = Sem(ctx, nc, "s4nm")
            s4x = Sem(ctx, nc, "s4x")
            s4m = Sem(ctx, nc, "s4m")
            s4e = Sem(ctx, nc, "s4e")
            s4s = Sem(ctx, nc, "s4s")
            s4w = Sem(ctx, nc, "s4w")

            with nc.psum_tensor("pbc4", [128, 2048], F32) as pbc4, \
                 nc.psum_tensor("pbc4b", [128, 2048], F32) as pbc4b:
                pbc = [pbc4, pbc4b]
                ph = Phase(nc)

                def f_sync(e):
                    e.dma_start(out=sbo[:], in_=bo[:]).then_inc(s4lw.h, 16)
                    for ci in range(8):
                        e.dma_start(out=WoT[:, ci * 1024:(ci + 1) * 1024],
                                    in_=wot[ci * 128:(ci + 1) * 128, :]).then_inc(s4lw.h, 16)
                ph.on("sync", f_sync)
                s4lw.v = 16 * 9

                def f_v0(e):
                    e.wait_ge(s_dn.h, 16 * H)
                    e.reciprocal(dnall[0:16, :], dnall[32:48, :]).then_inc(s4rv.h, 1)
                ph.on("vector", f_v0)

                def f_s0(e):
                    e.wait_ge(s4rv.h, 1)
                    e.activation(mn4[0:16, :], dnall[0:16, 0:1], AF.Identity
                                 ).then_inc(s4rv.h, 1)
                ph.on("scalar", f_s0)
                s4rv.v = 2

                def f_gp(e):
                    e.wait_ge(s4rv.h, 2)
                    for h in range(H):
                        if h >= 2:
                            e.wait_ge(s4bc.h, 2 + h - 1)
                        e.dma_start(out=rtmp[:, (h & 1) * TOWN:((h & 1) + 1) * TOWN],
                                    in_=dnall[h:h + 1, :]).then_inc(s4rg.h, 16)
                ph.on("gpsimd", f_gp)
                s4rg.v = 16 * H

                def f_t(e):
                    e.wait_ge(s4lw.h, 16)
                    for j in range(2):
                        e.matmul(pbc[0][:, j * 512:(j + 1) * 512], ones1[0:1, 0:128],
                                 sbo[0:1, j * 512:(j + 1) * 512], start=True, stop=True
                                 ).then_inc(s4bc.h, 1)
                    for h in range(H):
                        e.wait_ge(s4rg.h, 16 * (h + 1))
                        if h == 0:
                            e.wait_ge(s4nm.h, 1)
                        if h >= 2:
                            e.wait_ge(s4nm.h, h)
                        for j in range(4):
                            ins = e.matmul(pbc[h & 1][:, j * 512:(j + 1) * 512],
                                           ones1[0:1, 0:128],
                                           rtmp[0:1, (h & 1) * TOWN + j * 512:(h & 1) * TOWN + (j + 1) * 512],
                                           start=True, stop=True)
                            if j == 3:
                                ins.then_inc(s4bc.h, 1)
                ph.on("tensor", f_t)
                s4bc.v = 2 + H

                def f_v(e):
                    e.wait_ge(s4bc.h, 2)
                    e.tensor_copy(bob[:], pbc[0][:, 0:1024]).then_inc(s4nm.h, 1)
                    for h in range(H):
                        e.wait_ge(s4bc.h, 2 + h + 1)
                        e.tensor_tensor(QT[(h % 2) * 64:(h % 2) * 64 + 64,
                                           (h // 2) * TOWN:(h // 2 + 1) * TOWN],
                                        QT[(h % 2) * 64:(h % 2) * 64 + 64,
                                           (h // 2) * TOWN:(h // 2 + 1) * TOWN],
                                        pbc[h & 1][(h % 2) * 64:(h % 2) * 64 + 64, :],
                                        ALU.mult).then_inc(s4nm.h, 1)
                ph.on("vector", f_v)
                s4nm.v = 1 + H
                ph.emit()

            with nc.psum_tensor("Wop0", [128, 1024], F32) as Wop0, \
                 nc.psum_tensor("Wop1", [128, 1024], F32) as Wop1:
                Wop = [Wop0, Wop1]
                ph = Phase(nc)

                def f_sync(e):
                    for tb in range(NTB):
                        if tb >= 2:
                            e.wait_ge(s4e.h, tb - 1)
                        e.dma_start(out=xres[:, (tb & 1) * 1024:((tb & 1) + 1) * 1024],
                                    in_=xf[tb * 128:(tb + 1) * 128, :]).then_inc(s4x.h, 16)
                        if tb >= 1:
                            e.wait_ge(s4e.h, tb)
                            e.dma_start(out=x2s[(tb - 1) * 128:tb * 128, :],
                                        in_=x2t[:, ((tb - 1) & 1) * 1024:(((tb - 1) & 1) + 1) * 1024]
                                        ).then_inc(s4w.h, 16)
                    e.wait_ge(s4e.h, NTB)
                    e.dma_start(out=x2s[(NTB - 1) * 128:NTB * 128, :],
                                in_=x2t[:, ((NTB - 1) & 1) * 1024:(((NTB - 1) & 1) + 1) * 1024]
                                ).then_inc(s4w.h, 16)
                    e.wait_ge(s4w.h, 16 * NTB)
                ph.on("sync", f_sync)
                s4x.v = 16 * NTB
                s4w.v = 16 * NTB

                def f_t(e):
                    e.wait_ge(s4lw.h, 16 * 9)
                    for tb in range(NTB):
                        if tb >= 2:
                            e.wait_ge(s4e.h, tb - 1)
                        for hh in range(2):
                            for ci in range(8):
                                ins = e.matmul(
                                    Wop[tb & 1][:, hh * 512:(hh + 1) * 512],
                                    QT[:, ci * TOWN + tb * 128:ci * TOWN + (tb + 1) * 128],
                                    WoT[:, ci * 1024 + hh * 512:ci * 1024 + (hh + 1) * 512],
                                    start=(ci == 0), stop=(ci == 7))
                                if hh == 1 and ci == 7:
                                    ins.then_inc(s4m.h, 1)
                ph.on("tensor", f_t)
                s4m.v = NTB

                def f_v(e):
                    for tb in range(NTB):
                        sl = tb & 1
                        e.wait_ge(s4m.h, tb + 1)
                        e.wait_ge(s4x.h, 16 * (tb + 1))
                        if tb >= 2:
                            e.wait_ge(s4w.h, 16 * (tb - 1))
                        e.tensor_tensor(ya[:], Wop[sl][:], bob[:], ALU.add)
                        e.tensor_tensor(yb[:], ya[:],
                                        xres[:, sl * 1024:(sl + 1) * 1024], ALU.add)
                        e.reduce_sum(mn4[:], yb[:], AXX).then_inc(s4s.h, 1)
                        e.wait_ge(s4s.h, 6 * tb + 2)
                        e.tensor_scalar(xc4[:], yb[:], vr4[:], None, ALU.subtract)
                        e.tensor_tensor(ya[:], xc4[:], xc4[:], ALU.mult)
                        e.reduce_sum(mn4[:], ya[:], AXX).then_inc(s4s.h, 1)
                        e.wait_ge(s4s.h, 6 * tb + 4)
                        e.reciprocal(iv4[:], sd4[:]).then_inc(s4s.h, 1)
                        e.wait_ge(s4s.h, 6 * tb + 6)
                        e.tensor_scalar(x2t[:, sl * 1024:(sl + 1) * 1024], xc4[:],
                                        vr4[:], None, ALU.mult)
                        e.tensor_reduce(amaxc[:, tb:tb + 1],
                                        x2t[:, sl * 1024:(sl + 1) * 1024], AXX,
                                        ALU.max, apply_absolute_value=True
                                        ).then_inc(s4e.h, 1)
                ph.on("vector", f_v)
                s4e.v = NTB

                def f_sc(e):
                    for tb in range(NTB):
                        e.wait_ge(s4s.h, 6 * tb + 1)
                        e.activation(vr4[:], mn4[:], AF.Identity, scale=1.0 / C
                                     ).then_inc(s4s.h, 1)
                        e.wait_ge(s4s.h, 6 * tb + 3)
                        e.activation(sd4[:], mn4[:], AF.Sqrt, bias=sEps[:],
                                     scale=1.0 / C).then_inc(s4s.h, 1)
                        e.wait_ge(s4s.h, 6 * tb + 5)
                        e.activation(vr4[:], iv4[:], AF.Identity).then_inc(s4s.h, 1)
                ph.on("scalar", f_sc)
                s4s.v = 6 * NTB
                ph.emit()

        if dbg and upto == 4:
            _dump(nc, ctx, [(qtd, QT)])
        if upto == 4:
            return nc

        qts.close()

        with ExitStack() as p56:
            xqT = p56.enter_context(nc.sbuf_tensor("xqT", [128, 8 * TOWN], BF16))
            ccst = p56.enter_context(nc.sbuf_tensor("ccst", [128, 128], F32))
            csel = p56.enter_context(nc.sbuf_tensor("csel", [128, 1], F32))
            crow = p56.enter_context(nc.sbuf_tensor("crow", [1, 128], F32))
            g1 = p56.enter_context(nc.sbuf_tensor("g1", [1, 1], F32))
            x2q = p56.enter_context(nc.sbuf_tensor("x2q", [128, 2048], F32))
            qa = p56.enter_context(nc.sbuf_tensor("qa", [128, 1024], F32))
            qb = p56.enter_context(nc.sbuf_tensor("qb", [128, 1024], F32))
            xqt = p56.enter_context(nc.sbuf_tensor("xqt", [128, 2048], BF16))

            s5a = Sem(ctx, nc, "s5a")
            s5f = Sem(ctx, nc, "s5f")
            s5g = Sem(ctx, nc, "s5g")
            s4e_full = s4e.v
            s5l = Sem(ctx, nc, "s5l")
            s5c = Sem(ctx, nc, "s5c")
            s5b = Sem(ctx, nc, "s5b")
            s5t = Sem(ctx, nc, "s5t")
            s5x = Sem(ctx, nc, "s5x")
            s5q = Sem(ctx, nc, "s5q")
            s5z = Sem(ctx, nc, "s5z")

            cselb = p56.enter_context(nc.sbuf_tensor("cselb", [128, 1], F32))
            swfs6 = p56.enter_context(nc.sbuf_tensor("swfs6", [1, 1024], F32))
            sfb6 = p56.enter_context(nc.sbuf_tensor("sfb6", [1, 1024], F32))
            w8sb6 = p56.enter_context(nc.sbuf_tensor("w8sb6", [128, 8192], I8))
            w8bf_ = p56.enter_context(nc.sbuf_tensor("w8bf", [128, 8192], BF16))
            wfT_ = p56.enter_context(nc.sbuf_tensor("wfT", [128, 8192], BF16))
            s6l_ = Sem(ctx, nc, "s6l")
            s6c_ = Sem(ctx, nc, "s6c")
            s6t_ = Sem(ctx, nc, "s6t")
            sG = p56.enter_context(nc.sbuf_tensor("sG", [1, 1], F32))
            sScp = p56.enter_context(nc.sbuf_tensor("sScp", [1, 1], F32))
            psH = nc.psum_tensor("p5b", [128, 2], F32)
            p5b = psH.__enter__()
            if True:
                ph = Phase(nc)

                def f_s_fence(e):
                    e.wait_ge(s4e.h, s4e_full)
                    e.activation(ccst[:, 0:NTB], amaxc[:], AF.Identity
                                 ).then_inc(s5f.h, 1)
                ph.on("scalar", f_s_fence)
                s5f.v = 1

                def f_v_pre(e):
                    e.wait_ge(s5f.h, 1)
                    e.tensor_reduce(csel[:], amaxc[:], AXX, ALU.max).then_inc(s5a.h, 1)
                    e.wait_ge(s5a.h, 2)
                    e.memset(ccst[:], 0.0)
                    e.tensor_copy(ccst[:, 0:1], cselb[:]).then_inc(s5a.h, 1)
                ph.on("vector", f_v_pre)

                def f_s(e):
                    e.wait_ge(s5a.h, 1)
                    e.activation(cselb[:], csel[:], AF.Identity).then_inc(s5a.h, 1)
                    e.wait_ge(s5a.h, 4)
                    e.activation(cselb[:], csel[:], AF.Identity).then_inc(s5a.h, 1)
                    e.wait_ge(s5a.h, 6)
                    e.activation(sG[:], g1[:], AF.Identity, scale=1.0 / 127.0
                                 ).then_inc(s5a.h, 1)
                    e.wait_ge(s5a.h, 8)
                    e.activation(sScp[:], sS[:], AF.Identity).then_inc(s5a.h, 1)
                    e.wait_ge(s5b.h, 1)
                    e.copy(sRb[:], p5b[:, 0:1])
                    e.copy(sSb[:], p5b[:, 1:2]).then_inc(s5a.h, 1)
                ph.on("scalar", f_s)

                def f_sync(e):
                    e.dma_start(out=swfs6[:], in_=wfs[:]).then_inc(s6l_.h, 16)
                    e.dma_start(out=sfb6[:], in_=fbi[:]).then_inc(s6l_.h, 16)
                    for fj in range(8):
                        e.dma_start(out=w8sb6[:, fj * 1024:(fj + 1) * 1024],
                                    in_=wf8[fj * 128:(fj + 1) * 128, :]
                                    ).then_inc(s6l_.h, 16)
                    e.wait_ge(s5a.h, 3)
                    e.dma_start(out=ccin[:], in_=ccst[:]).then_inc(s5l.h, 16)
                ph.on("sync", f_sync)

                def f_gp(e):
                    e.wait_ge(s5l.h, 16)
                    e.collective_compute(
                        "AllReduce", ALU.max,
                        replica_groups=[[0, 1, 2, 3, 4, 5, 6, 7]],
                        ins=[ccin.ap().opt()], outs=[ccout.ap().opt()],
                    ).then_inc(s5c.h, 1)
                    e.wait_ge(s5c.h, 1)
                    e.dma_start(out=ccst[:], in_=ccout[:]).then_inc(s5l.h, 16)
                    e.wait_ge(s5g.h, 1)
                    e.dma_start(out=gtmp[:], in_=ccst[:, 20:21]).then_inc(s5l.h, 16)
                    e.wait_ge(s5l.h, 48)
                    e.dma_start(out=crow[:], in_=gtmp[:]).then_inc(s5l.h, 16)
                ph.on("gpsimd", f_gp)
                s5c.v = 1
                s5l.v = 64

                def f_t(e):
                    e.wait_ge(s5a.h, 10)
                    e.matmul(p5b[:, 0:1], ones1[0:1, 0:128], sRi[0:1, 0:1],
                             start=True, stop=True)
                    e.matmul(p5b[:, 1:2], ones1[0:1, 0:128], sS[0:1, 0:1],
                             start=True, stop=True).then_inc(s5b.h, 1)
                ph.on("tensor", f_t)
                s5b.v = 1
                s5a.v = 11

            def f_conv_early(e):
                e.wait_ge(s6l_.h, 16 * 10)
                for fj in range(8):
                    e.tensor_copy(w8bf_[:, fj * 1024:(fj + 1) * 1024],
                                  w8sb6[:, fj * 1024:(fj + 1) * 1024]).then_inc(s6c_.h, 1)
            ph.on("vector", f_conv_early)

            def f_v_post(e):
                e.wait_ge(s5l.h, 32)
                e.tensor_reduce(csel[:], ccst[:], AXX, ALU.max).then_inc(s5a.h, 1)
                e.wait_ge(s5a.h, 5)
                e.tensor_copy(ccst[:, 20:21], cselb[:]).then_inc(s5g.h, 1)
                e.wait_ge(s5l.h, 64)
                e.tensor_reduce(g1[:], crow[:], AXX, ALU.max).then_inc(s5a.h, 1)
                e.wait_ge(s5a.h, 7)
                e.tensor_scalar(sS[:], sG[:], 1.0, EPS, ALU.mult, ALU.max
                                ).then_inc(s5a.h, 1)
                e.wait_ge(s5a.h, 9)
                e.reciprocal(sRi[:], sScp[:]).then_inc(s5a.h, 1)
            ph.on("vector", f_v_post)

            def f_trans_early(e):
                for fj in range(8):
                    e.wait_ge(s6c_.h, fj + 1)
                    for ci in range(8):
                        e.dma_start_transpose(
                            wfT_[:, ci * 1024 + fj * 128:ci * 1024 + (fj + 1) * 128],
                            w8bf_[:, fj * 1024 + ci * 128:fj * 1024 + (ci + 1) * 128]
                        ).then_inc(s6t_.h, 16)
            ph.on("sync", f_trans_early)
            s6l_.v = 16 * 10
            s6c_.v = 8
            s6t_.v = 16 * 64


            def f_sync(e):
                def trans(tb):
                    e.wait_ge(s5q.h, tb + 1)
                    for ci in range(8):
                        e.dma_start_transpose(
                            xqT[:, ci * TOWN + tb * 128:ci * TOWN + (tb + 1) * 128],
                            xqt[:, (tb & 1) * 1024 + ci * 128:(tb & 1) * 1024 + (ci + 1) * 128]
                        ).then_inc(s5t.h, 16)
                for tb in range(NTB):
                    if tb >= 2:
                        e.wait_ge(s5q.h, tb - 1)
                    e.dma_start(out=x2q[:, (tb & 1) * 1024:((tb & 1) + 1) * 1024],
                                in_=x2s[tb * 128:(tb + 1) * 128, :]).then_inc(s5x.h, 16)
                    if tb >= 1:
                        trans(tb - 1)
                trans(NTB - 1)
            ph.on("sync", f_sync)
            s5x.v = 16 * NTB
            s5t.v = 16 * 8 * NTB

            def f_v(e):
                e.wait_ge(s5a.h, 11)
                for tb in range(NTB):
                    sl = tb & 1
                    e.wait_ge(s5x.h, 16 * (tb + 1))
                    if tb >= 2:
                        e.wait_ge(s5t.h, 16 * 8 * (tb - 1))
                    e.tensor_scalar(qa[:], x2q[:, sl * 1024:(sl + 1) * 1024],
                                    sRb[:], C3, ALU.mult, ALU.add)
                    e.tensor_scalar(qb[:], qa[:], C3, -128.0, ALU.subtract, ALU.max)
                    e.tensor_scalar(xqt[:, sl * 1024:(sl + 1) * 1024], qb[:],
                                    127.0, None, ALU.min).then_inc(s5q.h, 1)
            ph.on("vector", f_v)
            s5q.v = NTB

            if dbg:
                with nc.sbuf_tensor("dbgf", [128, 8 * TOWN], F32) as dbgf:
                    sdx = Sem(ctx, nc, "sdx")
                    ph = Phase(nc)

                    def f_dv(e):
                        for j in range(16):
                            e.tensor_copy(dbgf[:, j * 1024:(j + 1) * 1024],
                                          xqT[:, j * 1024:(j + 1) * 1024])
                        e.tensor_copy(dbgf[0:1, 0:1], sS[:])
                        e.tensor_copy(dbgf[0:1, 1:2], sRi[:]).then_inc(sdx.h, 1)
                    ph.on("vector", f_dv)

                    def f_ds(e):
                        e.wait_ge(sdx.h, 1)
                        e.dma_start(out=qtd[:], in_=dbgf[:]).then_inc(sdx.h, 16)
                        e.wait_ge(sdx.h, 17)
                    ph.on("sync", f_ds)
                    ph.emit()

            with ExitStack() as p6:
                w8bf = w8bf_
                wfT = wfT_
                cb6 = p6.enter_context(nc.sbuf_tensor("cb6", [128, 1024], F32))
                fb6 = p6.enter_context(nc.sbuf_tensor("fb6", [128, 1024], F32))
                x2r = p6.enter_context(nc.sbuf_tensor("x2r", [128, 2048], F32))
                ya6 = p6.enter_context(nc.sbuf_tensor("ya6", [128, 1024], F32))
                yb6 = p6.enter_context(nc.sbuf_tensor("yb6", [128, 1024], F32))
                xc6 = p6.enter_context(nc.sbuf_tensor("xc6", [128, 1024], F32))
                ot6 = p6.enter_context(nc.sbuf_tensor("ot6", [128, 2048], F32))
                mn6 = p6.enter_context(nc.sbuf_tensor("mn6", [128, 1], F32))
                vr6 = p6.enter_context(nc.sbuf_tensor("vr6", [128, 1], F32))
                sd6 = p6.enter_context(nc.sbuf_tensor("sd6", [128, 1], F32))
                iv6 = p6.enter_context(nc.sbuf_tensor("iv6", [128, 1], F32))

                s6l, s6c, s6t = s6l_, s6c_, s6t_
                s6b = Sem(ctx, nc, "s6b")
                s6g = Sem(ctx, nc, "s6g")
                s6x = Sem(ctx, nc, "s6x")
                s6m = Sem(ctx, nc, "s6m")
                s6e = Sem(ctx, nc, "s6e")
                s6s = Sem(ctx, nc, "s6s")
                s6o = Sem(ctx, nc, "s6o")

                with nc.psum_tensor("p6b", [128, 1024], F32) as p6bb, \
                     nc.psum_tensor("Fp0", [128, 1024], F32) as Fp0, \
                     nc.psum_tensor("Fp1", [128, 1024], F32) as Fp1:
                    Fp = [Fp0, Fp1]

                    def f_sync(e):
                        def store(tb):
                            e.wait_ge(s6e.h, tb + 1)
                            e.dma_start(out=out[tb * 128:(tb + 1) * 128, :],
                                        in_=ot6[:, (tb & 1) * 1024:((tb & 1) + 1) * 1024]
                                        ).then_inc(s6o.h, 16)
                        for tb in range(NTB):
                            if tb >= 2:
                                e.wait_ge(s6e.h, tb - 1)
                            e.dma_start(out=x2r[:, (tb & 1) * 1024:((tb & 1) + 1) * 1024],
                                        in_=x2s[tb * 128:(tb + 1) * 128, :]
                                        ).then_inc(s6x.h, 16)
                            if tb >= 1:
                                store(tb - 1)
                        store(NTB - 1)
                        e.wait_ge(s6o.h, 16 * NTB)
                    ph.on("sync", f_sync)
                    s6x.v = 16 * NTB
                    s6o.v = 16 * NTB

                    def f_t(e):
                        e.wait_ge(s6l.h, 32)
                        for j in range(2):
                            e.matmul(p6bb[:, j * 512:(j + 1) * 512], ones1[0:1, 0:128],
                                     swfs6[0:1, j * 512:(j + 1) * 512],
                                     start=True, stop=True).then_inc(s6b.h, 1)
                        e.wait_ge(s6g.h, 1)
                        for j in range(2):
                            e.matmul(p6bb[:, j * 512:(j + 1) * 512], ones1[0:1, 0:128],
                                     sfb6[0:1, j * 512:(j + 1) * 512],
                                     start=True, stop=True).then_inc(s6b.h, 1)
                        e.wait_ge(s6t.h, 16 * 64)
                        e.wait_ge(s5t.h, 16 * 8 * NTB)
                        for tb in range(NTB):
                            if tb >= 2:
                                e.wait_ge(s6e.h, tb - 1)
                            for hh in range(2):
                                for ci in range(8):
                                    ins = e.matmul(
                                        Fp[tb & 1][:, hh * 512:(hh + 1) * 512],
                                        xqT[:, ci * TOWN + tb * 128:ci * TOWN + (tb + 1) * 128],
                                        wfT[:, ci * 1024 + hh * 512:ci * 1024 + (hh + 1) * 512],
                                        start=(ci == 0), stop=(ci == 7))
                                    if hh == 1 and ci == 7:
                                        ins.then_inc(s6m.h, 1)
                    ph.on("tensor", f_t)
                    s6b.v = 4
                    s6m.v = NTB

                    def f_v(e):
                        e.wait_ge(s6b.h, 2)
                        e.tensor_scalar(cb6[:], p6bb[:], sSb[:], None, ALU.mult
                                        ).then_inc(s6g.h, 1)
                        e.wait_ge(s6b.h, 4)
                        e.tensor_copy(fb6[:], p6bb[:])
                        for tb in range(NTB):
                            sl = tb & 1
                            e.wait_ge(s6m.h, tb + 1)
                            e.wait_ge(s6x.h, 16 * (tb + 1))
                            if tb >= 2:
                                e.wait_ge(s6o.h, 16 * (tb - 1))
                            e.tensor_tensor(ya6[:], Fp[sl][:], cb6[:], ALU.mult)
                            e.tensor_tensor(yb6[:], ya6[:], fb6[:], ALU.add)
                            e.tensor_tensor(ya6[:], yb6[:],
                                            x2r[:, sl * 1024:(sl + 1) * 1024], ALU.add)
                            e.reduce_sum(mn6[:], ya6[:], AXX).then_inc(s6s.h, 1)
                            e.wait_ge(s6s.h, 6 * tb + 2)
                            e.tensor_scalar(xc6[:], ya6[:], vr6[:], None, ALU.subtract)
                            e.tensor_tensor(yb6[:], xc6[:], xc6[:], ALU.mult)
                            e.reduce_sum(mn6[:], yb6[:], AXX).then_inc(s6s.h, 1)
                            e.wait_ge(s6s.h, 6 * tb + 4)
                            e.reciprocal(iv6[:], sd6[:]).then_inc(s6s.h, 1)
                            e.wait_ge(s6s.h, 6 * tb + 6)
                            e.tensor_scalar(ot6[:, sl * 1024:(sl + 1) * 1024], xc6[:],
                                            vr6[:], None, ALU.mult).then_inc(s6e.h, 1)
                    ph.on("vector", f_v)
                    s6g.v = 1
                    s6e.v = NTB

                    def f_sc(e):
                        for tb in range(NTB):
                            e.wait_ge(s6s.h, 6 * tb + 1)
                            e.activation(vr6[:], mn6[:], AF.Identity, scale=1.0 / C
                                         ).then_inc(s6s.h, 1)
                            e.wait_ge(s6s.h, 6 * tb + 3)
                            e.activation(sd6[:], mn6[:], AF.Sqrt, bias=sEps[:],
                                         scale=1.0 / C).then_inc(s6s.h, 1)
                            e.wait_ge(s6s.h, 6 * tb + 5)
                            e.activation(vr6[:], iv6[:], AF.Identity).then_inc(s6s.h, 1)
                    ph.on("scalar", f_sc)
                    s6s.v = 6 * NTB
                    ph.emit()
            psH.__exit__(None, None, None)
        return nc
    return nc


def _dump(nc, ctx, pairs):
    sdm = Sem(ctx, nc, f"sdm{len(pairs)}")
    ph = Phase(nc)

    def f(e):
        n = 0
        for dst, src in pairs:
            e.dma_start(out=dst[:], in_=src[:]).then_inc(sdm.h, 16)
            n += 16
        e.wait_ge(sdm.h, n)
    ph.on("sync", f)
    ph.emit()



def _host_constants():
    slopes = _slopes()
    posv = np.arange(L, dtype=np.float32)[None, :]
    i = np.arange(128, dtype=np.float32)
    dgv = np.abs(i[:, None] - i[None, :]).astype(np.float32)
    coefv = np.zeros((32, 1), dtype=np.float32)
    for h in range(H):
        coefv[2 * h + 0, 0] = slopes[h]
        coefv[2 * h + 1, 0] = 2.0 * slopes[h]
    return posv, dgv, coefv


def make_in_maps(x, Wq, Wk, Wv, Wo, bov, W8, wsc, fbv):
    posv, dgv, coefv = _host_constants()
    wqT = np.ascontiguousarray(Wq.T)
    wkT = np.ascontiguousarray(Wk.T)
    wvT = np.ascontiguousarray(Wv.T)
    woT = np.ascontiguousarray(Wo.T)
    in_maps = []
    for c in range(NCORE):
        b, half = divmod(c, 2)
        xb = x[b] if half == 0 else x[b][::-1]
        xb = np.ascontiguousarray(xb)
        in_maps.append({
            "xf": xb, "xft": np.ascontiguousarray(xb.T),
            "wqt": wqT, "wkt": wkT, "wvt": wvT, "wot": woT,
            "bo": bov, "wf8": W8, "wfs": wsc, "fbi": fbv,
            "pos": posv, "dgi": dgv, "coef": coefv,
        })
    return in_maps


def kernel(L0_attn_q, L0_attn_Wq, L0_attn_Wk, L0_attn_Wv, L0_attn_Wo, L0_attn_bo,
           L0_ffn_weight_int8, L0_ffn_weight_scale, L0_ffn_bias):
    x = np.ascontiguousarray(np.asarray(L0_attn_q, dtype=np.float32))
    Wq = np.ascontiguousarray(np.asarray(L0_attn_Wq, dtype=np.float32))
    Wk = np.ascontiguousarray(np.asarray(L0_attn_Wk, dtype=np.float32))
    Wv = np.ascontiguousarray(np.asarray(L0_attn_Wv, dtype=np.float32))
    Wo = np.ascontiguousarray(np.asarray(L0_attn_Wo, dtype=np.float32))
    bov = np.ascontiguousarray(np.asarray(L0_attn_bo, dtype=np.float32).reshape(1, C))
    W8 = np.ascontiguousarray(np.asarray(L0_ffn_weight_int8, dtype=np.int8))
    wsc = np.ascontiguousarray(np.asarray(L0_ffn_weight_scale, dtype=np.float32).reshape(1, C))
    fbv = np.ascontiguousarray(np.asarray(L0_ffn_bias, dtype=np.float32).reshape(1, C))

    in_maps = make_in_maps(x, Wq, Wk, Wv, Wo, bov, W8, wsc, fbv)
    nc = build_nc()
    res = run_bass_kernel_spmd(nc, in_maps, core_ids=list(range(NCORE)))
    outv = np.empty((B, L, C), dtype=np.float32)
    for c in range(NCORE):
        b, half = divmod(c, 2)
        o = np.asarray(res.results[c]["out"])
        if half == 0:
            outv[b, 0:TOWN] = o
        else:
            outv[b, TOWN:L] = o[::-1]
    return outv
